# version 14
# baseline (speedup 1.0000x reference)
import math
import os
ATT_SERIAL = os.environ.get("MK_ATT_SERIAL", "0") == "1"
DBG_EPI = os.environ.get("MK_DBG_EPI", "0") == "1"
FLUSH_HOT = False
SHARED_AG = True
from contextlib import ExitStack

import numpy as np
import ml_dtypes

import concourse.bass as bass
import concourse.mybir as mybir
from concourse.bass_utils import run_bass_kernel_spmd
from concourse.aot_env import allow_ici_all_to_all

F32 = mybir.dt.float32
BF16 = mybir.dt.bfloat16
AF = mybir.ActivationFunctionType
ALU = mybir.AluOpType

D = 2048
H = 8192
S = 8192
NCORES = 8
TPC = S // NCORES
DC = D // 128
HCN = H // 128
QB = 512
NQB = S // QB
LN_EPS = 1e-5
LAMBDA_INIT = 0.8 - 0.6 * math.exp(-0.3 * 12)

ENGINES = ("tensor", "vector", "scalar", "gpsimd", "sync")


class Plan:
    def __init__(self, nc, stack):
        self.nc = nc
        self.stack = stack
        self.ops = {e: [] for e in ENGINES}
        self.sems = {}
        self.cnt = {}
        self.rec = {}

    def sem(self, name):
        if name not in self.sems:
            self.sems[name] = self.stack.enter_context(self.nc.semaphore(name))
            self.cnt[name] = 0
        return self.sems[name]

    def val(self, name):
        self.sem(name)
        return self.cnt[name]

    def on(self, eng, f):
        self.ops[eng].append(f)

    def wait(self, eng, name, value):
        if value is None or value <= 0:
            return
        s = self.sem(name)
        self.rec.setdefault(eng, []).append(("wait", name, value))
        self.on(eng, lambda e, s=s, v=value: e.wait_ge(s, v))

    def run(self, eng, f, inc=None, inc_by=1):
        if inc is None:
            self.on(eng, f)
            return None
        s = self.sem(inc)
        self.cnt[inc] += inc_by
        self.rec.setdefault(eng, []).append(("inc", inc, inc_by))
        self.on(eng, lambda e, s=s, f=f, k=inc_by: f(e).then_inc(s, k))
        return self.cnt[inc]

    ENG_SEM = {"tensor": "mm", "vector": "dve", "scalar": "act", "gpsimd": "gp"}

    def runf(self, eng, f):
        c = self.run(eng, f, inc=self.ENG_SEM[eng])
        self.wait(eng, self.ENG_SEM[eng], c)
        return c

    def dma(self, eng, out, in_, inc, **kw):
        s = self.sem(inc)
        self.cnt[inc] += 16
        self.rec.setdefault(eng, []).append(("inc", inc, 16))
        self.on(eng, lambda e, o=out, i=in_, s=s, kw=kw: e.dma_start(out=o, in_=i, **kw).then_inc(s, 16))
        return self.cnt[inc]

    def check_deadlock(self):
        ptr = {e: 0 for e in ENGINES}
        state = {n: 0 for n in self.cnt}
        progress = True
        while progress:
            progress = False
            for e in ENGINES:
                rec = self.rec.get(e, [])
                while ptr[e] < len(rec):
                    kind, name, v = rec[ptr[e]]
                    if kind == "wait" and state.get(name, 0) < v:
                        break
                    if kind == "inc":
                        state[name] = state.get(name, 0) + v
                    ptr[e] += 1
                    progress = True
        stuck = {e: ptr[e] for e in ENGINES if ptr[e] < len(self.rec.get(e, []))}
        if stuck:
            for e, i in stuck.items():
                kind, name, v = self.rec[e][i]
                print(f"DEADLOCK: {e} stuck at event {i}/{len(self.rec[e])}: "
                      f"{kind} {name} >= {v} (state={state.get(name, 0)})")
            raise RuntimeError("deadlock detected in plan")
        print("deadlock check: OK")

    def emit(self):
        nc = self.nc
        with nc.Block() as block:
            @block.tensor
            def _(e):
                for f in self.ops["tensor"]:
                    f(e)

            @block.vector
            def _(e):
                for f in self.ops["vector"]:
                    f(e)

            @block.scalar
            def _(e):
                for f in self.ops["scalar"]:
                    f(e)

            @block.gpsimd
            def _(e):
                for f in self.ops["gpsimd"]:
                    f(e)

            @block.sync
            def _(e):
                for f in self.ops["sync"]:
                    f(e)


class NS:
    pass


def alloc_persistent(p, stack):
    nc = p.nc
    b = NS()
    b.x = stack.enter_context(nc.sbuf_tensor("x_res", [128, DC * TPC], F32))
    b.ones_col = stack.enter_context(nc.sbuf_tensor("ones_col", [128, 1], F32))
    b.ones_col_bf = stack.enter_context(nc.sbuf_tensor("ones_col_bf", [128, 1], BF16))
    b.ones_row = stack.enter_context(nc.sbuf_tensor("ones_row", [1, 128], F32))
    b.rows = stack.enter_context(nc.sbuf_tensor("rows", [1, 5376], F32))
    b.tmp = stack.enter_context(nc.sbuf_tensor("tmp_f32", [128, 1024], F32))
    b.sq = stack.enter_context(nc.sbuf_tensor("sq_f32", [128, 1024], F32))
    return b


def plan_init(p, b):
    p.run("gpsimd", lambda e: e.memset(b.ones_col[:, :], 1.0), inc="init")
    p.run("gpsimd", lambda e: e.memset(b.ones_col_bf[:, :], 1.0), inc="init")
    p.run("gpsimd", lambda e: e.memset(b.ones_row[:, :], 1.0), inc="init")
    p.wait("tensor", "init", 3)
    p.wait("vector", "init", 3)


def plan_load_x(p, b, x_fm):
    src3 = x_fm[:, :].rearrange("(dc p) t -> p dc t", p=128)
    dst3 = b.x[:, :].rearrange("p (dc t) -> p dc t", t=TPC)
    engs = {0: ("sync", "scalar"), 1: ("gpsimd", "sync")}
    cnts = []
    for th in range(2):
        c = None
        for half in range(2):
            src = src3[:, half * 8:(half + 1) * 8, th * 512:(th + 1) * 512]
            dst = dst3[:, half * 8:(half + 1) * 8, th * 512:(th + 1) * 512]
            c = p.dma(engs[th][half], dst, src, "s_x%d" % th)
        cnts.append(c)
    return cnts


def plan_ln(p, b, a_dst, ps, th_list=(0, 1), xw=None, want_ev=False, bc=None,
            order="dc"):
    p.wait("tensor", "dve", p.val("dve"))
    p.wait("tensor", "act", p.val("act"))
    p.wait("scalar", "dve", p.val("dve"))
    p.wait("vector", "act", p.val("act"))
    last_apply = None
    stats_done = {}

    def emit_stats(th):
        t0 = th * 512
        s1 = ps[4 + th][0:1, :]
        s2 = ps[6 + th][0:1, :]
        if xw is not None and th == 1:
            p.wait("scalar", xw[0], xw[1])
            p.wait("tensor", xw[0], xw[1])
        p.wait("scalar", "mm", p.val("mm"))
        mm_s2 = [None] * DC
        for dc in range(DC):
            xs = b.x[:, dc * TPC + t0: dc * TPC + t0 + 512]
            sq_sl = b.sq[:, ((th * DC + dc) % 2) * 512:((th * DC + dc) % 2) * 512 + 512]
            if dc >= 2:
                p.wait("scalar", "mm", mm_s2[dc - 2])
            sq_done = p.run("scalar", lambda e, xs=xs, t=sq_sl: e.activation(t, xs, AF.Square), inc="act")
            p.wait("tensor", "act", sq_done)
            start, stop = dc == 0, dc == DC - 1
            p.run("tensor", lambda e, xs=xs, s1=s1, start=start, stop=stop: e.matmul(
                s1, b.ones_col[:, 0:1], xs, start=start, stop=stop), inc=None)
            mm_s2[dc] = p.run("tensor", lambda e, t=sq_sl, s2=s2, start=start, stop=stop: e.matmul(
                s2, b.ones_col[:, 0:1], t, start=start, stop=stop), inc="mm")
        stats_done[th] = mm_s2[DC - 1]

    def emit_rows(th):
        s1 = ps[4 + th][0:1, :]
        s2 = ps[6 + th][0:1, :]
        ro = th * 2560
        R_MEAN = b.rows[0:1, ro + 0:ro + 512]
        R_MSQ = b.rows[0:1, ro + 512:ro + 1024]
        R_VAR = b.rows[0:1, ro + 1024:ro + 1536]
        R_RSTD = b.rows[0:1, ro + 1536:ro + 2048]
        R_B = b.rows[0:1, ro + 2048:ro + 2560]
        p.wait("vector", "mm", stats_done[th])
        p.runf("vector", lambda e, s1=s1, R=R_MEAN: e.tensor_scalar(R, s1, 1.0 / D, None, ALU.mult))
        p.runf("vector", lambda e, s2=s2, R=R_MSQ: e.tensor_scalar(R, s2, 1.0 / D, None, ALU.mult))
        p.runf("vector", lambda e, RM=R_MEAN, RV=R_VAR: e.scalar_tensor_tensor(
            RV, RM, -1.0, RM, ALU.mult, ALU.mult))
        c = p.runf("vector", lambda e, RV=R_VAR, RM=R_MSQ: e.tensor_tensor(RV, RV, RM, ALU.add))
        p.wait("scalar", "dve", c)
        p.runf("scalar", lambda e, RV=R_VAR: e.activation(RV, RV, AF.Ln, bias=np.float32(LN_EPS)))
        c = p.run("scalar", lambda e, RV=R_VAR, RR=R_RSTD: e.activation(
            RR, RV, AF.Exp, scale=np.float32(-0.5)), inc="act")
        p.wait("vector", "act", c)
        c = p.runf("vector", lambda e, RM=R_MEAN, RR=R_RSTD, RB=R_B: e.scalar_tensor_tensor(
            RB, RM, -1.0, RR, ALU.mult, ALU.mult))
        return c, R_RSTD, R_B

    def emit_bc(th, c, R_RSTD, R_B):
        bcA = ps[th * 2][:, :]
        bcB = ps[th * 2 + 1][:, :]
        p.wait("tensor", "dve", c)
        p.run("tensor", lambda e, bcA=bcA, RR=R_RSTD: e.matmul(
            bcA, b.ones_row[0:1, :], RR, start=True, stop=True), inc=None)
        mm_bc = p.run("tensor", lambda e, bcB=bcB, RB=R_B: e.matmul(
            bcB, b.ones_row[0:1, :], RB, start=True, stop=True), inc="mm")
        p.wait("vector", "mm", mm_bc)
        p.run("vector", lambda e, th=th, bcA=bcA: e.tensor_copy(
            bc[:, (th * 2) * 512:(th * 2) * 512 + 512], bcA), inc="dve")
        return p.run("vector", lambda e, th=th, bcB=bcB: e.tensor_copy(
            bc[:, (th * 2 + 1) * 512:(th * 2 + 1) * 512 + 512], bcB), inc="dve")

    def emit_applies():
        nonlocal last_apply
        evs = {}
        if order == "dc":
            pairs = [(dc, thi) for dc in range(DC) for thi in range(len(th_list))]
        else:
            pairs = [(dc, thi) for thi in range(len(th_list)) for dc in range(DC)]
        for idx, (dc, thi) in enumerate(pairs):
            th = th_list[thi]
            if True:
                if xw is not None and th == 1 and dc == 0:
                    p.wait("vector", xw[0], xw[1])
                t0 = th * 512
                bcA = bc[:, (th * 2) * 512:(th * 2) * 512 + 512]
                bcB = bc[:, (th * 2 + 1) * 512:(th * 2 + 1) * 512 + 512]
                xs = b.x[:, dc * TPC + t0: dc * TPC + t0 + 512]
                asl = a_dst[:, dc * TPC + t0: dc * TPC + t0 + 512]
                sl = idx % 2
                tsl = b.tmp[:, sl * 512:sl * 512 + 512]
                p.runf("vector", lambda e, xs=xs, t=tsl, bcA=bcA: e.tensor_tensor(t, xs, bcA, ALU.mult))
                last_apply = p.run("vector", lambda e, asl=asl, t=tsl, bcB=bcB: e.tensor_tensor(
                    asl, t, bcB, ALU.add), inc="dve")
                evs[(dc, th)] = last_apply
        return evs

    for th in th_list:
        emit_stats(th)
    rowres = {}
    for th in th_list:
        rowres[th] = emit_rows(th)
    for th in th_list:
        emit_bc(th, *rowres[th])
    evs = emit_applies()
    if want_ev:
        return last_apply, evs
    return last_apply


def plan_ffn(p, b, sb, a_bf, dram, first_call, a_ready_dve, between_th=None,
             out_stream=None, a_ev=None):
    nc = p.nc
    wgu = sb["wgu"]
    NB = TPC // 512
    pname = "1" if first_call else "2"

    def wsl(wpar, mat, dc):
        base = ((wpar * 2) + mat) * D
        return wgu[:, base + dc * 128: base + dc * 128 + 128]

    if a_ev is None:
        p.wait("tensor", "dve", a_ready_dve)
    if not first_call:
        p.wait("gpsimd", "mmo", p.val("mmo"))
    with ExitStack() as _ps:
        ps_g = [[_ps.enter_context(nc.psum_tensor("psg%d%d" % (par, nb) + pname, [128, 512], F32))
                 for nb in range(2)] for par in range(2)]
        ps_u = [[_ps.enter_context(nc.psum_tensor("psu%d%d" % (par, nb) + pname, [128, 512], F32))
                 for nb in range(2)] for par in range(2)]
        mm_done = [None] * HCN
        ev_done = [None] * HCN
        st_done = [None] * HCN
        if a_ev is None:
            p.wait("tensor", "act", p.val("act"))
        for h in range(HCN):
            par = h % 2
            wpar = h % 4
            if h >= 4:
                p.wait("gpsimd", "mmg", mm_done[h - 4])
            w_cnt = None
            for mat, wname in enumerate(("wgT", "wuT")):
                wt = dram[wname]
                src = wt[h * 128:(h + 1) * 128, :]
                dst = wgu[:, ((wpar * 2) + mat) * D: ((wpar * 2) + mat) * D + D]
                w_cnt = p.dma("gpsimd", dst, src, "w%d" % wpar + pname)
            if first_call:
                hc = h
                pp = hc % 2
                prep = sb["prep"]
                if hc >= 2:
                    p.wait("gpsimd", "prep_st%d" % pp, 64 * (hc // 2))
                ld = p.dma("gpsimd", prep[:, pp * D:(pp + 1) * D],
                           dram["wdT"][hc * 128:(hc + 1) * 128, :], "prep_ld%d" % pp)
                p.wait("sync", "prep_ld%d" % pp, ld)
                g8i, sl = hc // 8, hc % 8
                for qq in range(4):
                    p.dma("sync", dram["wd_bf"][(qq * 8 + g8i) * 128:(qq * 8 + g8i + 1) * 128,
                                                sl * 512:(sl + 1) * 512],
                          prep[:, pp * D + qq * 512: pp * D + (qq + 1) * 512], "prep_st%d" % pp)
            p.wait("tensor", "w%d" % wpar + pname, w_cnt)
            if h >= 2:
                p.wait("tensor", "dve", ev_done[h - 2])
            for mat in range(2):
                pst = (ps_g if mat == 0 else ps_u)[par]
                for dc in range(DC):
                    for nb in range(NB):
                        if a_ev is not None and h == 0 and mat == 0:
                            p.wait("tensor", "dve", a_ev[(dc, nb)])
                        start, stop = dc == 0, dc == DC - 1
                        last = mat == 1 and nb == NB - 1 and dc == DC - 1
                        f = lambda e, pst=pst, nb=nb, wpar=wpar, mat=mat, dc=dc, start=start, stop=stop: e.matmul(
                            pst[nb][:, :], wsl(wpar, mat, dc),
                            a_bf[:, dc * TPC + nb * 512: dc * TPC + (nb + 1) * 512],
                            start=start, stop=stop)
                        if last:
                            mm_done[h] = p.run("tensor", f, inc="mmg")
                        else:
                            p.run("tensor", f, inc=None)
            p.wait("scalar", "mmg", mm_done[h])
            if h >= 2:
                p.wait("scalar", "dve", ev_done[h - 2])
            for nb in range(NB):
                c = p.run("scalar", lambda e, par=par, nb=nb: e.activation(
                    sb["silu"][:, par * TPC + nb * 512: par * TPC + (nb + 1) * 512],
                    ps_g[par][nb][:, :], AF.Silu), inc="act")
            p.wait("vector", "act", c)
            if h >= 2:
                p.wait("vector", "g_st%d" % par + pname, st_done[h - 2])
            for nb in range(NB):
                ev_done[h] = p.run("vector", lambda e, par=par, nb=nb: e.scalar_tensor_tensor(
                    sb["gated"][:, par * TPC + nb * 512: par * TPC + (nb + 1) * 512],
                    ps_u[par][nb][:, :], 0.5,
                    sb["silu"][:, par * TPC + nb * 512: par * TPC + (nb + 1) * 512],
                    ALU.mult, ALU.mult), inc="dve")
            p.wait("sync", "dve", ev_done[h])
            g4i, sl4 = h // 4, h % 4
            st_done[h] = p.dma("sync", dram["gated"][g4i * 128:(g4i + 1) * 128,
                                                     sl4 * TPC:(sl4 + 1) * TPC],
                               sb["gated"][:, par * TPC:(par + 1) * TPC], "g_st%d" % par + pname)
        gu_last_ev = ev_done[HCN - 1]
    gst_tot = (p.val("g_st0" + pname), p.val("g_st1" + pname))

    wd_ready = (p.val("prep_st0"), p.val("prep_st1"))
    p.wait("tensor", "dve", gu_last_ev)
    gst_tot = (p.val("g_st0" + pname), p.val("g_st1" + pname))
    mmd_base = p.val("mmd")
    bank_guard = {}
    split = between_th is not None
    groups = [((0,), q) for q in range(4)] + [((1,), q) for q in range(4)] if split \
        else [((0, 1), q) for q in range(4)]
    with ExitStack() as _ps:
        ps_d = [_ps.enter_context(nc.psum_tensor("psd%d" % i + pname, [128, 512], F32))
                for i in range(8)]
        wd_in = sb["wd_in"]
        g_in = sb["g_in"]
        mmd_per_hc = 4 * (1 if split else 2) // 4
        for gi, (ths, q) in enumerate(groups):
            nth = len(ths)
            if split:
                bs = (gi % 2) * 4 if gi < 4 else 0
            else:
                bs = 0
            wldc = [None] * 8
            mm_hc = [None] * HCN
            for g8 in range(8):
                li = gi * 8 + g8
                if gi == 0 and g8 == 0 and first_call:
                    p.wait("scalar", "prep_st0", wd_ready[0])
                    p.wait("scalar", "prep_st1", wd_ready[1])
                prev = li - 2
                if prev >= 0:
                    p.wait("scalar", "mmd", mmd_base + (prev // 8) * HCN + (prev % 8) * 8 + 8)
                srcw = dram["wd_bf"][(q * 8 + g8) * 128:(q * 8 + g8 + 1) * 128, :]
                dstw = wd_in[:, (li % 2) * 4096:(li % 2) * 4096 + 4096]
                wldc[g8] = p.dma("scalar", dstw, srcw, "wd%d" % (li % 2) + pname)
            gldc = [None] * 16
            for g4 in range(16):
                lj = gi * 16 + g4
                if gi == 0 and g4 == 0:
                    p.wait("sync", "g_st0" + pname, gst_tot[0])
                    p.wait("sync", "g_st1" + pname, gst_tot[1])
                prev = lj - 2
                if prev >= 0:
                    p.wait("sync", "mmd", mmd_base + (prev // 16) * HCN + (prev % 16) * 4 + 4)
                if split:
                    th = ths[0]
                    srcg = dram["gated"][g4 * 128:(g4 + 1) * 128, :].rearrange(
                        "p (s t) -> p s t", t=TPC)[:, :, th * 512:(th + 1) * 512]
                    dstg = g_in[:, (lj % 2) * 2048:(lj % 2) * 2048 + 2048].rearrange(
                        "p (s t) -> p s t", t=512)
                else:
                    srcg = dram["gated"][g4 * 128:(g4 + 1) * 128, :]
                    dstg = g_in[:, (lj % 2) * 4096:(lj % 2) * 4096 + 4096]
                gldc[g4] = p.dma("sync", dstg, srcg, "gl%d" % (lj % 2) + pname)
            for hc in range(HCN):
                li = gi * 8 + hc // 8
                lj = gi * 16 + hc // 4
                if hc % 8 == 0:
                    p.wait("tensor", "wd%d" % (li % 2) + pname, wldc[hc // 8])
                if hc % 4 == 0:
                    p.wait("tensor", "gl%d" % (lj % 2) + pname, gldc[hc // 4])
                if hc == 0:
                    p.wait("tensor", "dve", bank_guard.get(bs, 0))
                gw = (lj % 2) * (2048 if split else 4096)
                for dt in range(4):
                    for thi, th in enumerate(ths):
                        start, stop = hc == 0, hc == HCN - 1
                        lastmm = dt == 3 and thi == nth - 1
                        if split:
                            rh = g_in[:, gw + (hc % 4) * 512: gw + (hc % 4) * 512 + 512]
                        else:
                            rh = g_in[:, gw + (hc % 4) * 1024 + th * 512:
                                      gw + (hc % 4) * 1024 + th * 512 + 512]
                        f = lambda e, bs=bs, li=li, hc=hc, dt=dt, thi=thi, nth=nth, rh=rh, start=start, stop=stop: e.matmul(
                            ps_d[bs + dt * nth + thi][:, :],
                            wd_in[:, (li % 2) * 4096 + (hc % 8) * 512 + dt * 128:
                                  (li % 2) * 4096 + (hc % 8) * 512 + dt * 128 + 128],
                            rh, start=start, stop=stop)
                        if lastmm:
                            mm_hc[hc] = p.run("tensor", f, inc="mmd")
                        else:
                            p.run("tensor", f, inc=None)
            p.wait("vector", "mmd", mm_hc[HCN - 1])
            for dt in range(4):
                for thi, th in enumerate(ths):
                    xsl = b.x[:, (q * 4 + dt) * TPC + th * 512:(q * 4 + dt) * TPC + (th + 1) * 512]
                    p.run("vector", lambda e, xsl=xsl, bs=bs, dt=dt, thi=thi, nth=nth: e.tensor_tensor(
                        xsl, ps_d[bs + dt * nth + thi][:, :], xsl, ALU.add), inc="dve")
            bank_guard[bs] = p.val("dve")
            if out_stream is not None:
                ev = p.val("dve")
                osrc = b.x[:, q * 4 * TPC:(q + 1) * 4 * TPC].rearrange(
                    "p (dc t) -> p dc t", t=TPC)
                odst = out_stream[:, :].rearrange(
                    "(dc p) t -> p dc t", p=128)[:, q * 4:(q + 1) * 4, :]
                if gi < len(groups) - 1:
                    p.wait("gpsimd", "dve", ev)
                    p.dma("gpsimd", odst, osrc, "s_out")
                else:
                    for hf, eng in enumerate(("gpsimd", "sync")):
                        p.wait(eng, "dve", ev)
                        p.dma(eng, odst[:, hf * 2:(hf + 1) * 2, :],
                              osrc[:, hf * 2:(hf + 1) * 2, :], "s_out")
            if split and gi == 3 and between_th is not None:
                between_th(ps_d)()
    return p.val("dve")


def plan_store_a(p, a_bf, dst_dram, th, eng, ev):
    p.wait(eng, "dve", ev)
    src = a_bf[:, :].rearrange("p (dc t) -> p dc t", t=TPC)[:, :, th * 512:(th + 1) * 512]
    dst = dst_dram[:, :].rearrange("p (dc c) -> p dc c", c=512)
    return p.dma(eng, dst, src, "a_st%d" % th)


def plan_collective(p, kind, ins, outs, sem_name, wait_dma=None):
    if wait_dma is not None:
        p.wait("gpsimd", wait_dma[0], wait_dma[1])
    s = p.sem(sem_name)
    p.cnt[sem_name] += 1
    p.rec.setdefault("gpsimd", []).append(("inc", sem_name, 1))

    def f(e, kind=kind, ins=ins, outs=outs, s=s):
        if kind == "AllToAll":
            with allow_ici_all_to_all():
                return e.collective_compute(kind, ALU.bypass, replica_groups=[list(range(NCORES))],
                                            ins=ins, outs=outs).then_inc(s)
        return e.collective_compute(kind, ALU.bypass, replica_groups=[list(range(NCORES))],
                                    ins=ins, outs=outs).then_inc(s)
    p.on("gpsimd", f)
    return p.cnt[sem_name]


def plan_attn_prep(p, b, at, dram):
    LB = 4608
    Lr = lambda a, z: b.rows[0:1, LB + a:LB + z]
    c_lam = p.dma("sync", Lr(0, 512), dram["lam_row"][:, :], "prep2")
    c_gn = p.dma("sync", at["gbpad"][0:4, :], dram["gn4"][:, :], "prep2")
    c_msk = p.dma("sync", at["masks"][:, :], dram["consts_bf"][:, :], "prep2")
    c_idf = p.dma("sync", at["identf"][:, :], dram["ident_f32"][:, :], "prep2")
    p.wait("vector", "prep2", c_idf)
    c = p.runf("vector", lambda e: e.tensor_tensor(Lr(512, 640), Lr(0, 128), Lr(128, 256), ALU.mult))
    c = p.runf("vector", lambda e: e.tensor_tensor(Lr(640, 768), Lr(256, 384), Lr(384, 512), ALU.mult))
    p.wait("scalar", "dve", c)
    c = p.run("scalar", lambda e: e.activation(Lr(0, 256), Lr(512, 768), AF.Exp), inc="act")
    p.wait("vector", "act", c)
    c = p.runf("vector", lambda e: e.tensor_tensor(Lr(256, 384), Lr(0, 128), Lr(128, 256), ALU.subtract))
    lam_done = p.runf("vector", lambda e: e.tensor_scalar(
        Lr(256, 384), Lr(256, 384), LAMBDA_INIT, None, ALU.add))
    at["lam_row128"] = Lr(256, 384)
    p.wait("tensor", "prep2", c_idf)
    c = p.run("tensor", lambda e: e.transpose(at["ps_sp"][:, 0:128], at["gbpad"][:, :], at["identf"][:, :]), inc="mA")
    p.wait("tensor", "dve", lam_done)
    c2 = p.run("tensor", lambda e: e.transpose(at["ps_sp"][:, 128:129], at["lam_row128"],
                                               at["identf"][0:1, 0:1]), inc="mA")
    p.wait("vector", "mA", c2)
    p.runf("vector", lambda e: e.tensor_scalar(
        at["gb"][:, :], at["ps_sp"][:, 0:4], 1.0 - LAMBDA_INIT, None, ALU.mult))
    p.runf("vector", lambda e: e.tensor_copy(at["lamc"][:, :], at["ps_sp"][:, 128:129]))
    return c_idf


def plan_qkv_head(p, b, at, hl, dram, psq, ag1_done):
    p.wait("gpsimd", "mq", p.val("mq"))
    src = dram["wqkvT"][hl * 128:(hl + 1) * 128, :]
    dst = at["wq"][:, :]
    wq_cnt = p.dma("gpsimd", dst, src, "wqkv")
    p.wait("sync", "a_st0", dram.get("_a_st_a"))
    p.wait("sync", "a_st1", dram.get("_a_st_b"))
    p.wait("sync", "mq", p.val("mq"))
    p.wait("tensor", "prep2", p.val("prep2"))
    p.wait("vector", "h_st0", p.val("h_st0"))
    p.wait("vector", "h_st1", p.val("h_st1"))
    ev_v = [None] * 16
    tp_done = [None] * 16
    mq_done = [None] * 16
    order = [2 * i for i in range(8)] + [2 * i + 1 for i in range(8)]
    for oi, tc in enumerate(order):
        pi = oi % 2
        r, half = tc // 2, tc % 2
        if oi >= 2:
            p.wait("sync", "mq", mq_done[order[oi - 2]])
        if oi == 0:
            p.wait("sync", "cc1a", ag1_done[0])
        if oi == 8:
            p.wait("sync", "cc1b", ag1_done[1])
        srcn = dram["ag1_out_" + ("a" if half == 0 else "b")]
        src = srcn[r * 128:(r + 1) * 128, :].rearrange("p (dc c) -> p dc c", c=512)
        dst = at["chunk"][:, pi * 8192:(pi + 1) * 8192].rearrange("p (dc t) -> p dc t", t=512)
        ld = p.dma("sync", dst, src, "aq%d" % pi)
        p.wait("tensor", "aq%d" % pi, ld)
        if oi == 0:
            p.wait("tensor", "wqkv", wq_cnt)
        if oi >= 2:
            p.wait("tensor", "dve", ev_v[order[oi - 2]])
            p.wait("tensor", "act", p.val("act"))
        mq = None
        for mat in range(3):
            T = psq["T"][mat][pi]
            for dc in range(DC):
                start, stop = dc == 0, dc == DC - 1
                f = lambda e, T=T, pi=pi, mat=mat, dc=dc, start=start, stop=stop: e.matmul(
                    T[:, :],
                    at["wq"][:, dc * 384 + mat * 128: dc * 384 + mat * 128 + 128],
                    at["chunk"][:, pi * 8192 + dc * 512: pi * 8192 + (dc + 1) * 512],
                    start=start, stop=stop)
                last = mat == 2 and dc == DC - 1
                if last:
                    mq = p.run("tensor", f, inc="mq")
                else:
                    p.run("tensor", f, inc=None)
        mq_done[tc] = mq
        p.wait("vector", "mq", mq)
        p.run("vector", lambda e, tc=tc, pi=pi: e.tensor_copy(
            at["q_fm"][:, tc * 512:(tc + 1) * 512], psq["T"][0][pi][:, :]), inc=None)
        if oi >= 2:
            p.wait("vector", "mq", tp_done[order[oi - 2]][0])
        ev_v[tc] = p.run("vector", lambda e, tc=tc, pi=pi: e.tensor_copy(
            at["v_st"][:, pi * 512:(pi + 1) * 512], psq["T"][2][pi][:, :]), inc="dve")
        p.wait("scalar", "mq", mq)
        p.run("scalar", lambda e, tc=tc, pi=pi: e.activation(
            at["k_fm"][:, tc * 512:(tc + 1) * 512], psq["T"][1][pi][:, :], AF.Copy,
            bias=0.0), inc="act")
        p.wait("tensor", "dve", ev_v[tc])
        if oi >= 2:
            p.wait("tensor", "dve", tp_done[order[oi - 2]][1])
        tmm = None
        for i in range(4):
            tmm = p.run("tensor", lambda e, pi=pi, i=i: e.transpose(
                psq["ps_t"][pi][:, i * 128:(i + 1) * 128],
                at["v_st"][:, pi * 512 + i * 128: pi * 512 + (i + 1) * 128],
                at["ident_bf"][:, :]), inc="mq")
        p.wait("vector", "mq", tmm)
        tcopy = p.run("vector", lambda e, tc=tc, pi=pi: e.tensor_copy(
            at["v_tm"][:, tc * 512:(tc + 1) * 512], psq["ps_t"][pi][:, 0:512]), inc="dve")
        tp_done[tc] = (tmm, tcopy)
    return tp_done[order[-1]][1]


def plan_attn_head(p, b, at, hl, dram, psa, qkv_done, dbg=None):
    sA, sB, aC, dD = psa
    scale = np.float32(0.125)
    p.wait("tensor", "dve", qkv_done)
    p.wait("tensor", "act", p.val("act"))
    p.wait("scalar", "mA", p.val("mA"))
    p.wait("scalar", "dve", p.val("dve"))
    p.wait("vector", "mA", p.val("mA"))
    e2 = {}
    m2 = {}
    s_cnt = {}
    av_cnt = {}
    acc_cnt = {}
    gof = [0]
    st_prev = [None, None]
    ev_state = {"a1": None, "a2": None}
    NEB = 6

    def emit_scores(j, kv, diag_m):
        g = gof[0] + kv
        sp = kv % 2
        qsl = slice(j * QB, (j + 1) * QB)
        if g >= 2 and (g - 2) in e2:
            p.wait("tensor", "act", e2[g - 2])
        for br in range(2):
            ps = (sA if br == 0 else sB)[sp][:, :]
            lh = at["k_fm"][br * 64:(br + 1) * 64, kv * 128:(kv + 1) * 128]
            rh = at["q_fm"][br * 64:(br + 1) * 64, qsl]
            f = lambda e, ps=ps, lh=lh, rh=rh, stop=diag_m is None: e.matmul(
                ps, lh, rh, start=True, stop=stop)
            if diag_m is None and br == 1:
                s_cnt[g] = p.run("tensor", f, inc="mA")
            else:
                p.run("tensor", f, inc=None)
            if diag_m is not None:
                fm = lambda e, ps=ps, m=diag_m: e.matmul(
                    ps, at["tri_bf"][:, :],
                    at["masks"][:, m * 512:(m + 1) * 512], start=False, stop=True)
                if br == 1:
                    s_cnt[g] = p.run("tensor", fm, inc="mA")
                else:
                    p.run("tensor", fm, inc=None)

    def emit_exp(j, kv, diag_m):
        g = gof[0] + kv
        sp = kv % 2
        bi = g % NEB
        p.wait("scalar", "mA", s_cnt[g])
        if g >= NEB and (g - NEB) in av_cnt:
            p.wait("scalar", "mA", av_cnt[g - NEB])
            p.wait("scalar", "dve", acc_cnt[g - NEB])
        p.run("scalar", lambda e, sp=sp, bi=bi: e.activation(
            at["exp"][:, bi * 512:bi * 512 + 512],
            sA[sp][:, :], AF.Exp, scale=scale), inc="act")
        e2[g] = p.run("scalar", lambda e, sp=sp, bi=bi: e.activation(
            at["exp"][:, (NEB + bi) * 512:(NEB + bi) * 512 + 512],
            sB[sp][:, :], AF.Exp, scale=scale), inc="act")
        p.wait("vector", "act", e2[g])
        rv = (lambda eng, f: p.runf(eng, f)) if FLUSH_HOT else (lambda eng, f: p.run(eng, f, inc="dve"))
        for br in range(2):
            esl = at["exp"][:, (br * NEB + bi) * 512:(br * NEB + bi) * 512 + 512]
            accsl = at["acc"][:, br * 512:(br + 1) * 512]
            if kv == 0:
                p.wait("vector", "dve", ev_state.get("c2"))
                c = rv("vector", lambda e, accsl=accsl, esl=esl: e.tensor_copy(accsl, esl))
            else:
                c = rv("vector", lambda e, accsl=accsl, esl=esl: e.tensor_tensor(
                    accsl, accsl, esl, ALU.add))
            if br == 1:
                acc_cnt[g] = c

    def emit_av(j, kv, diag, last):
        g = gof[0] + kv
        bi = g % NEB
        p.wait("tensor", "act", e2[g])
        if kv == 0:
            p.wait("tensor", "dve", ev_state["a2"])
        for br in range(2):
            ps = aC[br][:, :]
            lh = at["v_tm"][:, kv * 128:(kv + 1) * 128]
            rh = at["exp"][:, (br * NEB + bi) * 512:(br * NEB + bi) * 512 + 512]
            f = lambda e, ps=ps, lh=lh, rh=rh, start=kv == 0, stop=last: e.matmul(
                ps, lh, rh, start=start, stop=stop)
            if br == 1:
                av_cnt[g] = p.run("tensor", f, inc="mA")
            else:
                p.run("tensor", f, inc=None)

    def make_epilogue(j):
        A1s = b.tmp[:, 0:512]
        A2s = b.tmp[:, 512:1024]
        rho_b = b.sq[:, 0:512]
        osq = b.sq[:, 512:1024]
        R1 = b.rows[0:1, 0:512]
        R2 = b.rows[0:1, 512:1024]
        R3 = b.rows[0:1, 1024:1536]
        R4 = b.rows[0:1, 1536:2048]
        R5 = b.rows[0:1, 2048:2560]
        R6 = b.rows[0:1, 2560:3072]
        R7 = b.rows[0:1, 3072:3584]
        st = {}

        av_last = av_cnt[gof[0] - 1]
        p.wait("vector", "mA", av_last)
        ev_state["a1"] = None
        c = p.runf("vector", lambda e: e.tensor_copy(A1s, aC[0][:, :]))
        ev_state["a1"] = c
        ev_state["a2"] = p.runf("vector", lambda e: e.tensor_copy(A2s, aC[1][:, :]))
        p.run("vector", lambda e: e.tensor_copy(at["accbf"][:, 0:512], at["acc"][:, 0:512]), inc=None)
        st["c2"] = p.run("vector", lambda e: e.tensor_copy(at["accbf"][:, 512:1024], at["acc"][:, 512:1024]), inc="dve")
        ev_state["c2"] = st["c2"]

        def round1():
            p.wait("tensor", "dve", st["c2"])
            p.run("tensor", lambda e: e.matmul(dD[0][0:1, :], b.ones_col_bf[:, 0:1],
                                               at["accbf"][:, 0:512], start=True, stop=True), inc=None)
            cd = p.run("tensor", lambda e: e.matmul(dD[1][0:1, :], b.ones_col_bf[:, 0:1],
                                                    at["accbf"][:, 512:1024], start=True, stop=True), inc="mA")
            p.wait("scalar", "mA", cd)
            p.runf("scalar", lambda e: e.activation(R1, dD[0][0:1, :], AF.Ln))
            c = p.run("scalar", lambda e: e.activation(R2, dD[1][0:1, :], AF.Ln), inc="act")
            p.wait("vector", "act", c)
            c = p.runf("vector", lambda e: e.tensor_tensor(R2, R1, R2, ALU.subtract))
            p.wait("scalar", "dve", c)
            st["rho"] = p.run("scalar", lambda e: e.activation(R2, R2, AF.Exp), inc="act")

        def round2():
            p.wait("tensor", "act", st["rho"])
            crb = p.run("tensor", lambda e: e.matmul(dD[0][:, :], b.ones_row[0:1, :], R2,
                                                     start=True, stop=True), inc="mA")
            p.wait("vector", "mA", crb)
            p.runf("vector", lambda e: e.tensor_copy(rho_b, dD[0][:, :]))
            p.runf("vector", lambda e: e.tensor_tensor(A2s, A2s, rho_b, ALU.mult))
            p.runf("vector", lambda e: e.tensor_scalar(A2s, A2s, at["lamc"][:, 0:1], None, ALU.mult))
            p.runf("vector", lambda e: e.tensor_tensor(A1s, A1s, A2s, ALU.subtract))
            st["o"] = p.runf("vector", lambda e: e.tensor_tensor(osq, A1s, A1s, ALU.mult))

        def round3():
            p.wait("tensor", "dve", st["o"])
            p.run("tensor", lambda e: e.matmul(dD[0][0:1, :], b.ones_col[:, 0:1], A1s,
                                               start=True, stop=True), inc=None)
            cs = p.run("tensor", lambda e: e.matmul(dD[1][0:1, :], b.ones_col[:, 0:1], osq,
                                                    start=True, stop=True), inc="mA")
            p.wait("vector", "mA", cs)
            p.runf("vector", lambda e: e.tensor_scalar(R3, dD[0][0:1, :], 1.0 / 128.0, None, ALU.mult))
            p.runf("vector", lambda e: e.tensor_scalar(R4, dD[1][0:1, :], 1.0 / 128.0, None, ALU.mult))
            p.runf("vector", lambda e: e.scalar_tensor_tensor(R5, R3, -1.0, R3, ALU.mult, ALU.mult))
            cv = p.runf("vector", lambda e: e.tensor_tensor(R5, R5, R4, ALU.add))
            p.wait("scalar", "dve", cv)
            p.runf("scalar", lambda e: e.activation(R6, R5, AF.Ln, bias=np.float32(LN_EPS)))
            cq = p.run("scalar", lambda e: e.activation(R6, R6, AF.Exp, scale=np.float32(-0.5)), inc="act")
            p.wait("vector", "act", cq)
            st["gb"] = p.runf("vector", lambda e: e.scalar_tensor_tensor(R7, R3, -1.0, R6, ALU.mult, ALU.mult))

        def round4():
            p.wait("tensor", "dve", st["gb"])
            p.run("tensor", lambda e: e.matmul(dD[0][:, :], b.ones_row[0:1, :], R6,
                                               start=True, stop=True), inc=None)
            cab = p.run("tensor", lambda e: e.matmul(dD[1][:, :], b.ones_row[0:1, :], R7,
                                                     start=True, stop=True), inc="mA")
            p.wait("vector", "mA", cab)
            p.runf("vector", lambda e: e.tensor_tensor(at["on"][:, 0:512], A1s, dD[0][:, :], ALU.mult))
            p.runf("vector", lambda e: e.tensor_tensor(at["on"][:, 0:512], at["on"][:, 0:512], dD[1][:, :], ALU.add))
            pj = j % 2
            if st_prev[pj] is not None:
                p.wait("vector", "h_st%d" % pj, st_prev[pj])
            cfin = p.run("vector", lambda e, pj=pj: e.tensor_scalar(
                at["head_bf"][:, pj * 512:(pj + 1) * 512], at["on"][:, 0:512],
                at["gb"][:, hl * 2:hl * 2 + 1], at["gb"][:, hl * 2 + 1:hl * 2 + 2],
                ALU.mult, ALU.add), inc="dve")
            p.wait("sync", "dve", cfin)
            st_prev[pj] = p.dma("sync", dram["a2a_in"][
                                (j // 2) * 256 + hl * 128:(j // 2) * 256 + hl * 128 + 128,
                                (j % 2) * 512:(j % 2) * 512 + 512],
                                at["head_bf"][:, pj * 512:(pj + 1) * 512], "h_st%d" % pj)
        return [round1, round2, round3, round4]

    pend = []
    for j in range(NQB):
        ktiles = 4 * (j + 1)
        for kv in range(ktiles):
            diag_m = kv - 4 * j if kv >= 4 * j else None
            emit_scores(j, kv, diag_m)
            emit_exp(j, kv, diag_m)
            if kv >= 3:
                emit_av(j, kv - 3, (kv - 3) >= 4 * j, False)
            if pend and kv >= 1:
                pend.pop(0)()
        for t in (3, 2):
            emit_av(j, ktiles - t, (ktiles - t) >= 4 * j, False)
        emit_av(j, ktiles - 1, True, True)
        while pend:
            pend.pop(0)()
        gof[0] += ktiles
        pend = make_epilogue(j)
    while pend:
        pend.pop(0)()
    return (p.val("h_st0"), p.val("h_st1"))



def plan_wo(p, b, dram, a2a_done):
    nc = p.nc
    with nc.sbuf_tensor("wo_rhs", [128, DC * TPC], BF16) as wo_rhs, \
         nc.sbuf_tensor("wo_w", [128, 2 * D], BF16) as wo_w, \
         nc.psum_tensor("pso0", [128, 1024], F32) as o0, \
         nc.psum_tensor("pso1", [128, 1024], F32) as o1:
        pso = [o0, o1]
        src3 = dram["a2a_out"][:, :].rearrange("(hc p) t -> p hc t", p=128)
        dst3 = wo_rhs[:, :].rearrange("p (hc t) -> p hc t", t=TPC)
        rhs_cnt = None
        for half, eng in enumerate(("sync", "scalar")):
            p.wait(eng, "cc2", 1)
            rhs_cnt = p.dma(eng, dst3[:, half * 8:(half + 1) * 8, :],
                            src3[:, half * 8:(half + 1) * 8, :], "wo_rhs_ld")
        mmo = [None] * DC
        evo = [None] * DC
        for do in range(DC):
            par = do % 2
            if do >= 2:
                p.wait("gpsimd", "mmo", mmo[do - 2])
            src = dram["woT"][do * 128:(do + 1) * 128, :]
            dst = wo_w[:, par * D:(par + 1) * D]
            wcnt = p.dma("gpsimd", dst, src, "wo_w_ld%d" % par)
            p.wait("tensor", "wo_w_ld%d" % par, wcnt)
            if do == 0:
                p.wait("tensor", "wo_rhs_ld", rhs_cnt)
                p.wait("tensor", "dve", p.val("dve"))
                p.wait("tensor", "act", p.val("act"))
            if do >= 2:
                p.wait("tensor", "dve", evo[do - 2])
            for hci, hc in enumerate(range(DC)):
                for nb in range(2):
                    start, stop = hci == 0, hci == DC - 1
                    last = nb == 1 and hci == DC - 1
                    f = lambda e, par=par, nb=nb, hc=hc, do=do, start=start, stop=stop: e.matmul(
                        pso[par][:, nb * 512:(nb + 1) * 512],
                        wo_w[:, par * D + hc * 128: par * D + hc * 128 + 128],
                        wo_rhs[:, hc * TPC + nb * 512: hc * TPC + (nb + 1) * 512],
                        start=start, stop=stop)
                    if last:
                        mmo[do] = p.run("tensor", f, inc="mmo")
                    else:
                        p.run("tensor", f, inc=None)
            p.wait("vector", "mmo", mmo[do])
            xsl = b.x[:, do * TPC:(do + 1) * TPC]
            evo[do] = p.run("vector", lambda e, xsl=xsl, par=par: e.tensor_tensor(
                xsl, pso[par][:, :], xsl, ALU.add), inc="dve")
    return p.val("dve")


def build_full(stop_after="full", skip_ffn1=False):
    nc = bass.Bass("TRN2", target_bir_lowering=False, debug=False)
    dram = {}
    x_fm = nc.dram_tensor("x_fm", [D, TPC], F32, kind="ExternalInput").ap()
    dram["wgT"] = nc.dram_tensor("wgT", [HCN * 128, D], F32, kind="ExternalInput").ap()
    dram["wuT"] = nc.dram_tensor("wuT", [HCN * 128, D], F32, kind="ExternalInput").ap()
    dram["wdT"] = nc.dram_tensor("wdT", [H, D], F32, kind="ExternalInput").ap()
    dram["wqkvT"] = nc.dram_tensor("wqkvT", [2 * 128, DC * 384], F32, kind="ExternalInput").ap()
    dram["woT"] = nc.dram_tensor("woT", [DC * 128, D], F32, kind="ExternalInput").ap()
    dram["lam_row"] = nc.dram_tensor("lam_row", [1, 512], F32, kind="ExternalInput").ap()
    dram["gn4"] = nc.dram_tensor("gn4", [4, 128], F32, kind="ExternalInput").ap()
    dram["consts_bf"] = nc.dram_tensor("consts_bf", [128, 4 * 512 + 256], BF16, kind="ExternalInput").ap()
    dram["ident_f32"] = nc.dram_tensor("ident_f32", [128, 128], F32, kind="ExternalInput").ap()
    out_fm = nc.dram_tensor("out_fm", [D, TPC], F32, kind="ExternalOutput").ap()
    dbg = nc.dram_tensor("dbg", [3 * 128, S], BF16, kind="ExternalOutput").ap()         if stop_after in ("qkv", "attn", "ag1") else None

    dram["gated"] = nc.dram_tensor("gated_scr", [16 * 128, 4 * TPC], BF16).ap()
    dram["wd_bf"] = nc.dram_tensor("wd_bf_scr", [32 * 128, 8 * 512], BF16).ap()
    kw = {"addr_space": "Shared"} if SHARED_AG else {}
    for hf in "ab":
        dram["ag1_in_" + hf] = nc.dram_tensor("ag1_in_" + hf, [128, DC * 512], BF16).ap()
        dram["ag1_out_" + hf] = nc.dram_tensor("ag1_out_" + hf, [NCORES * 128, DC * 512], BF16, **kw).ap()
    dram["a2a_in"] = nc.dram_tensor("a2a_in", [2 * NCORES * 128, TPC], BF16).ap()
    dram["a2a_out"] = nc.dram_tensor("a2a_out", [2 * NCORES * 128, TPC], BF16).ap()

    with ExitStack() as stack:
        p = Plan(nc, stack)
        b = alloc_persistent(p, stack)
        plan_init(p, b)
        xc = plan_load_x(p, b, x_fm)
        for e in ("scalar", "tensor", "vector"):
            p.wait(e, "s_x0", xc[0])

        with ExitStack() as f1:
            a_bf = f1.enter_context(nc.sbuf_tensor("a1_bf", [128, DC * TPC], BF16))
            sb = {
                "wgu": f1.enter_context(nc.sbuf_tensor("wgu1", [128, 4 * 2 * D], BF16)),
                "silu": f1.enter_context(nc.sbuf_tensor("silu1", [128, 2 * TPC], BF16)),
                "gated": f1.enter_context(nc.sbuf_tensor("gated1", [128, 2 * TPC], BF16)),
                "prep": f1.enter_context(nc.sbuf_tensor("prep1", [128, 2 * D], BF16)),
                "wd_in": f1.enter_context(nc.sbuf_tensor("wd_in1", [128, 2 * 8 * 512], BF16)),
                "g_in": f1.enter_context(nc.sbuf_tensor("g_in1", [128, 2 * 4 * 1024], BF16)),
            }
            with ExitStack() as _ps:
                lps = [_ps.enter_context(nc.psum_tensor("lnp%d" % i, [128, 512], F32)) for i in range(8)]
                bc1 = sb["g_in"][:, 0:4096].bitcast(F32)
                a_ready, a_ev1 = plan_ln(p, b, a_bf, lps, xw=("s_x1", xc[1]),
                                         want_ev=True, bc=bc1)
            if not skip_ffn1:
                plan_ffn(p, b, sb, a_bf, dram, first_call=True, a_ready_dve=a_ready,
                         a_ev=a_ev1)
            with ExitStack() as _ps:
                lps = [_ps.enter_context(nc.psum_tensor("lnq%d" % i, [128, 512], F32)) for i in range(8)]
                a2_ready, a_ev2 = plan_ln(p, b, a_bf, lps, want_ev=True, bc=bc1,
                                          order="th")
            a_st_a = plan_store_a(p, a_bf, dram["ag1_in_a"], 0, "sync", a_ev2[(DC - 1, 0)])
            ag1a = plan_collective(p, "AllGather", [dram["ag1_in_a"][:, :]],
                                   [dram["ag1_out_a"][:, :]], "cc1a",
                                   wait_dma=("a_st0", a_st_a))
            a_st_b = plan_store_a(p, a_bf, dram["ag1_in_b"], 1, "gpsimd", a2_ready)
            dram["_a_st_a"] = a_st_a
            dram["_a_st_b"] = a_st_b
            if stop_after == "ffn1":
                finish(p, b, out_fm)
                return nc
        ag1_done = (ag1a,
                    plan_collective(p, "AllGather", [dram["ag1_in_b"][:, :]],
                                    [dram["ag1_out_b"][:, :]], "cc1b",
                                    wait_dma=("a_st1", a_st_b)))

        with ExitStack() as f2:
            at = {
                "wq": f2.enter_context(nc.sbuf_tensor("wq", [128, DC * 384], BF16)),
                "chunk": f2.enter_context(nc.sbuf_tensor("chunk", [128, 2 * 8192], BF16)),
                "q_fm": f2.enter_context(nc.sbuf_tensor("q_fm", [128, S], BF16)),
                "k_fm": f2.enter_context(nc.sbuf_tensor("k_fm", [128, S], BF16)),
                "v_tm": f2.enter_context(nc.sbuf_tensor("v_tm", [128, S], BF16)),
                "v_st": f2.enter_context(nc.sbuf_tensor("v_st", [128, 1024], BF16)),
                "exp": f2.enter_context(nc.sbuf_tensor("exp_sb", [128, 12 * 512], BF16)),

                "masks": f2.enter_context(nc.sbuf_tensor("masks", [128, 4 * 512 + 256], BF16)),
                "identf": f2.enter_context(nc.sbuf_tensor("identf", [128, 128], F32)),
                "gbpad": f2.enter_context(nc.sbuf_tensor("gbpad", [128, 128], F32)),
                "gb": f2.enter_context(nc.sbuf_tensor("gb", [128, 4], F32)),
                "lamc": f2.enter_context(nc.sbuf_tensor("lamc", [128, 1], F32)),
            }
            at["ident_bf"] = at["masks"][:, 4 * 512:4 * 512 + 128]
            at["tri_bf"] = at["masks"][:, 4 * 512 + 128:4 * 512 + 256]
            at["head_bf"] = at["v_st"]
            at["o"] = at["chunk"][:, 0:2048].bitcast(F32)
            at["on"] = at["chunk"][:, 2048:4096].bitcast(F32)
            at["accbf"] = at["chunk"][:, 4096:5120]
            at["acc"] = at["chunk"][:, 5120:7168].bitcast(F32)
            for hl in range(2):
                if hl == 0:
                    with nc.psum_tensor("qs_prep", [128, 512], F32) as ps_sp:
                        at["ps_sp"] = ps_sp
                        c_msk = plan_attn_prep(p, b, at, dram)
                        p.wait("vector", "prep2", c_msk)
                with ExitStack() as _ps:
                    psq = {"T": [[_ps.enter_context(nc.psum_tensor("q%d%d_%d" % (m, pi, hl), [128, 512], F32))
                                  for pi in range(2)] for m in range(3)],
                           "ps_t": [_ps.enter_context(nc.psum_tensor("qt%d_%d" % (pi, hl), [128, 1024], BF16))
                                    for pi in range(2)]}
                    p.wait("tensor", "dve", p.val("dve"))
                    p.wait("tensor", "act", p.val("act"))
                    p.wait("tensor", "mA", p.val("mA"))
                    qkv_done = plan_qkv_head(p, b, at, hl, dram, psq, ag1_done)
                if stop_after == "qkv":
                    for i, t in enumerate(("q_fm", "k_fm", "v_tm")):
                        p.wait("sync", "dve", p.val("dve"))
                        p.wait("sync", "act", p.val("act"))
                        p.dma("sync", dbg[i * 128:(i + 1) * 128, :], at[t][:, :], "s_out")
                    finish(p, b, out_fm)
                    return nc
                with ExitStack() as _ps:
                    sA = [_ps.enter_context(nc.psum_tensor("aA%d_%d" % (i, hl), [128, 512], F32)) for i in range(2)]
                    sB = [_ps.enter_context(nc.psum_tensor("aB%d_%d" % (i, hl), [128, 512], F32)) for i in range(2)]
                    aC = [_ps.enter_context(nc.psum_tensor("aC%d_%d" % (i, hl), [128, 512], F32)) for i in range(2)]
                    dD = [_ps.enter_context(nc.psum_tensor("aD%d_%d" % (i, hl), [128, 512], F32)) for i in range(2)]
                    plan_attn_head(p, b, at, hl, dram, (sA, sB, aC, dD), qkv_done, dbg=dbg)
            p.wait("gpsimd", "h_st0", p.val("h_st0"))
            p.wait("gpsimd", "h_st1", p.val("h_st1"))
            plan_collective(p, "AllToAll", [dram["a2a_in"][:, :]],
                            [dram["a2a_out"][:, :]], "cc2", wait_dma=None)
        plan_wo(p, b, dram, None)
        if stop_after == "wo":
            finish(p, b, out_fm)
            return nc

        with ExitStack() as f3:
            a_bf = f3.enter_context(nc.sbuf_tensor("a3_bf", [128, DC * TPC], BF16))
            sb = {
                "wgu": f3.enter_context(nc.sbuf_tensor("wgu2", [128, 4 * 2 * D], BF16)),
                "silu": f3.enter_context(nc.sbuf_tensor("silu2", [128, 2 * TPC], BF16)),
                "gated": f3.enter_context(nc.sbuf_tensor("gated2", [128, 2 * TPC], BF16)),
                "prep": None,
                "wd_in": f3.enter_context(nc.sbuf_tensor("wd_in2", [128, 2 * 8 * 512], BF16)),
                "g_in": f3.enter_context(nc.sbuf_tensor("g_in2", [128, 2 * 4 * 1024], BF16)),
            }
            p.wait("scalar", "dve", p.val("dve"))
            p.wait("tensor", "dve", p.val("dve"))
            with ExitStack() as _ps:
                lps = [_ps.enter_context(nc.psum_tensor("lnr%d" % i, [128, 512], F32)) for i in range(8)]
                bc3 = sb["g_in"][:, 0:4096].bitcast(F32)
                a3_ready, a_ev3 = plan_ln(p, b, a_bf, lps, want_ev=True, bc=bc3)
            fin = plan_ffn(p, b, sb, a_bf, dram, first_call=False, a_ready_dve=a3_ready,
                           out_stream=out_fm, a_ev=a_ev3)

        finish(p, b, out_fm, streamed=True)
    return nc


def finish(p, b, out_fm, streamed=False):
    p.wait("sync", "dve", p.val("dve"))
    p.wait("sync", "act", p.val("act"))
    if not streamed:
        src = b.x[:, :].rearrange("p (dc t) -> p dc t", t=TPC)
        dst = out_fm[:, :].rearrange("(dc p) t -> p dc t", p=128)
        p.dma("sync", dst, src, "s_out")
    p.wait("sync", "s_out", p.val("s_out"))
    p.check_deadlock()
    p.emit()


def make_host_inputs(inp):
    x = np.asarray(inp["L0_attn_x"])[0]
    W_qkv = np.asarray(inp["L0_attn_W_qkv"])
    W_o = np.asarray(inp["L0_attn_W_o"])
    wg = np.asarray(inp["L0_ffn_w_gate"])
    wu = np.asarray(inp["L0_ffn_w_up"])
    wd = np.asarray(inp["L0_ffn_w_down"])
    lq1 = np.asarray(inp["L0_attn_lambda_q1"])
    lk1 = np.asarray(inp["L0_attn_lambda_k1"])
    lq2 = np.asarray(inp["L0_attn_lambda_q2"])
    lk2 = np.asarray(inp["L0_attn_lambda_k2"])
    gamma = np.asarray(inp["L0_attn_gn_gamma"])
    beta = np.asarray(inp["L0_attn_gn_beta"])

    def tile_w(w):
        ho = w.shape[0] // 128
        t = w.reshape(ho, 128, w.shape[1] // 128, 128).transpose(0, 3, 2, 1)
        return np.ascontiguousarray(t.reshape(ho * 128, w.shape[1]))
    wgT = tile_w(wg)
    wuT = tile_w(wu)
    wdT = np.ascontiguousarray(wd.T)
    woT = tile_w(W_o)
    lam_row = np.concatenate([lq1, lk1, lq2, lk2]).astype(np.float32).reshape(1, 512)
    masks = np.zeros((128, 4 * 512 + 256), dtype=ml_dtypes.bfloat16)
    r = np.arange(128)[:, None]
    c = np.arange(512)[None, :]
    for m in range(4):
        masks[:, m * 512:(m + 1) * 512] = np.where((128 * m + r) > c, -400.0, 0.0).astype(ml_dtypes.bfloat16)
    masks[:, 4 * 512:4 * 512 + 128] = np.eye(128, dtype=ml_dtypes.bfloat16)
    masks[:, 4 * 512 + 128:4 * 512 + 256] = (r <= np.arange(128)[None, :]).astype(ml_dtypes.bfloat16)
    ident_f32 = np.eye(128, dtype=np.float32)

    qkv3 = W_qkv.reshape(3, 16, 128, D)
    in_maps = []
    for core in range(NCORES):
        xs = np.ascontiguousarray(x[core * TPC:(core + 1) * TPC, :].T)
        blk = np.empty((2 * 128, DC * 384), dtype=np.float32)
        for hl in range(2):
            h = 2 * core + hl
            for mat in range(3):
                t = qkv3[mat, h].reshape(128, DC, 128).transpose(2, 1, 0)
                blk[hl * 128:(hl + 1) * 128, :].reshape(128, DC, 3, 128)[:, :, mat, :] = t
        gn4 = np.stack([gamma[2 * core], beta[2 * core],
                        gamma[2 * core + 1], beta[2 * core + 1]]).astype(np.float32)
        in_maps.append({
            "x_fm": xs, "wgT": wgT, "wuT": wuT, "wdT": wdT,
            "wqkvT": np.ascontiguousarray(blk), "woT": woT,
            "lam_row": lam_row, "gn4": gn4,
            "consts_bf": masks, "ident_f32": ident_f32,
        })
    return in_maps


def kernel(**inputs):
    nc = build_full(os.environ.get("MK_STOP", "full"))
    in_maps = make_host_inputs(inputs)
    res = run_bass_kernel_spmd(nc, in_maps, core_ids=list(range(NCORES)),
                               trace=os.environ.get("MK_TRACE", "0") == "1")
    out = np.empty((1, S, D), dtype=np.float32)
    for core in range(NCORES):
        out[0, core * TPC:(core + 1) * TPC, :] = res.results[core]["out_fm"].T
    return out


def np_ln(x):
    x = x.astype(np.float64)
    mu = x.mean(-1, keepdims=True)
    v = x.var(-1, keepdims=True)
    return (x - mu) / np.sqrt(v + LN_EPS)


def np_ffn_res(x, wg, wu, wd):
    xn = np_ln(x)
    g = xn @ wg.T.astype(np.float64)
    u = xn @ wu.T.astype(np.float64)
    gated = (g / (1.0 + np.exp(-g))) * u
    return x + 0.5 * (gated @ wd.T.astype(np.float64))



if __name__ == "__main__":
    pass


# version 15
# speedup vs baseline: 1.0629x; 1.0629x over previous
import math
import os
ATT_SERIAL = os.environ.get("MK_ATT_SERIAL", "0") == "1"
DBG_EPI = os.environ.get("MK_DBG_EPI", "0") == "1"
FLUSH_HOT = False
SHARED_AG = True
from contextlib import ExitStack

import numpy as np
import ml_dtypes

import concourse.bass as bass
import concourse.mybir as mybir
from concourse.bass_utils import run_bass_kernel_spmd
from concourse.aot_env import allow_ici_all_to_all

F32 = mybir.dt.float32
BF16 = mybir.dt.bfloat16
AF = mybir.ActivationFunctionType
ALU = mybir.AluOpType

D = 2048
H = 8192
S = 8192
NCORES = 8
TPC = S // NCORES
DC = D // 128
HCN = H // 128
QB = 512
NQB = S // QB
LN_EPS = 1e-5
LAMBDA_INIT = 0.8 - 0.6 * math.exp(-0.3 * 12)

ENGINES = ("tensor", "vector", "scalar", "gpsimd", "sync")


class Plan:
    def __init__(self, nc, stack):
        self.nc = nc
        self.stack = stack
        self.ops = {e: [] for e in ENGINES}
        self.sems = {}
        self.cnt = {}
        self.rec = {}

    def sem(self, name):
        if name not in self.sems:
            self.sems[name] = self.stack.enter_context(self.nc.semaphore(name))
            self.cnt[name] = 0
        return self.sems[name]

    def val(self, name):
        self.sem(name)
        return self.cnt[name]

    def on(self, eng, f):
        self.ops[eng].append(f)

    def wait(self, eng, name, value):
        if value is None or value <= 0:
            return
        s = self.sem(name)
        self.rec.setdefault(eng, []).append(("wait", name, value))
        self.on(eng, lambda e, s=s, v=value: e.wait_ge(s, v))

    def run(self, eng, f, inc=None, inc_by=1):
        if inc is None:
            self.on(eng, f)
            return None
        s = self.sem(inc)
        self.cnt[inc] += inc_by
        self.rec.setdefault(eng, []).append(("inc", inc, inc_by))
        self.on(eng, lambda e, s=s, f=f, k=inc_by: f(e).then_inc(s, k))
        return self.cnt[inc]

    ENG_SEM = {"tensor": "mm", "vector": "dve", "scalar": "act", "gpsimd": "gp"}

    def runf(self, eng, f):
        c = self.run(eng, f, inc=self.ENG_SEM[eng])
        self.wait(eng, self.ENG_SEM[eng], c)
        return c

    def dma(self, eng, out, in_, inc, **kw):
        s = self.sem(inc)
        self.cnt[inc] += 16
        self.rec.setdefault(eng, []).append(("inc", inc, 16))
        self.on(eng, lambda e, o=out, i=in_, s=s, kw=kw: e.dma_start(out=o, in_=i, **kw).then_inc(s, 16))
        return self.cnt[inc]

    def check_deadlock(self):
        ptr = {e: 0 for e in ENGINES}
        state = {n: 0 for n in self.cnt}
        progress = True
        while progress:
            progress = False
            for e in ENGINES:
                rec = self.rec.get(e, [])
                while ptr[e] < len(rec):
                    kind, name, v = rec[ptr[e]]
                    if kind == "wait" and state.get(name, 0) < v:
                        break
                    if kind == "inc":
                        state[name] = state.get(name, 0) + v
                    ptr[e] += 1
                    progress = True
        stuck = {e: ptr[e] for e in ENGINES if ptr[e] < len(self.rec.get(e, []))}
        if stuck:
            for e, i in stuck.items():
                kind, name, v = self.rec[e][i]
                print(f"DEADLOCK: {e} stuck at event {i}/{len(self.rec[e])}: "
                      f"{kind} {name} >= {v} (state={state.get(name, 0)})")
            raise RuntimeError("deadlock detected in plan")
        print("deadlock check: OK")

    def emit(self):
        nc = self.nc
        with nc.Block() as block:
            @block.tensor
            def _(e):
                for f in self.ops["tensor"]:
                    f(e)

            @block.vector
            def _(e):
                for f in self.ops["vector"]:
                    f(e)

            @block.scalar
            def _(e):
                for f in self.ops["scalar"]:
                    f(e)

            @block.gpsimd
            def _(e):
                for f in self.ops["gpsimd"]:
                    f(e)

            @block.sync
            def _(e):
                for f in self.ops["sync"]:
                    f(e)


class NS:
    pass


def alloc_persistent(p, stack):
    nc = p.nc
    b = NS()
    b.x = stack.enter_context(nc.sbuf_tensor("x_res", [128, DC * TPC], F32))
    b.ones_col = stack.enter_context(nc.sbuf_tensor("ones_col", [128, 1], F32))
    b.ones_col_bf = stack.enter_context(nc.sbuf_tensor("ones_col_bf", [128, 1], BF16))
    b.ones_row = stack.enter_context(nc.sbuf_tensor("ones_row", [1, 128], F32))
    b.rows = stack.enter_context(nc.sbuf_tensor("rows", [1, 5376], F32))
    b.tmp = stack.enter_context(nc.sbuf_tensor("tmp_f32", [128, 1024], F32))
    b.sq = stack.enter_context(nc.sbuf_tensor("sq_f32", [128, 1024], F32))
    return b


def plan_init(p, b):
    p.run("gpsimd", lambda e: e.memset(b.ones_col[:, :], 1.0), inc="init")
    p.run("gpsimd", lambda e: e.memset(b.ones_col_bf[:, :], 1.0), inc="init")
    p.run("gpsimd", lambda e: e.memset(b.ones_row[:, :], 1.0), inc="init")
    p.wait("tensor", "init", 3)
    p.wait("vector", "init", 3)


def plan_load_x(p, b, x_fm):
    src3 = x_fm[:, :].rearrange("(dc p) t -> p dc t", p=128)
    dst3 = b.x[:, :].rearrange("p (dc t) -> p dc t", t=TPC)
    engs = {0: ("sync", "scalar"), 1: ("gpsimd", "sync")}
    cnts = []
    for th in range(2):
        c = None
        for half in range(2):
            src = src3[:, half * 8:(half + 1) * 8, th * 512:(th + 1) * 512]
            dst = dst3[:, half * 8:(half + 1) * 8, th * 512:(th + 1) * 512]
            c = p.dma(engs[th][half], dst, src, "s_x%d" % th)
        cnts.append(c)
    return cnts


def plan_ln(p, b, a_dst, ps, th_list=(0, 1), xw=None, want_ev=False, bc=None,
            order="dc"):
    p.wait("tensor", "dve", p.val("dve"))
    p.wait("tensor", "act", p.val("act"))
    p.wait("scalar", "dve", p.val("dve"))
    p.wait("vector", "act", p.val("act"))
    last_apply = None
    stats_done = {}

    def emit_stats(th):
        t0 = th * 512
        s1 = ps[4 + th][0:1, :]
        s2 = ps[6 + th][0:1, :]
        if xw is not None and th == 1:
            p.wait("scalar", xw[0], xw[1])
            p.wait("tensor", xw[0], xw[1])
        p.wait("scalar", "mm", p.val("mm"))
        mm_s2 = [None] * DC
        for dc in range(DC):
            xs = b.x[:, dc * TPC + t0: dc * TPC + t0 + 512]
            sq_sl = b.sq[:, ((th * DC + dc) % 2) * 512:((th * DC + dc) % 2) * 512 + 512]
            if dc >= 2:
                p.wait("scalar", "mm", mm_s2[dc - 2])
            sq_done = p.run("scalar", lambda e, xs=xs, t=sq_sl: e.activation(t, xs, AF.Square), inc="act")
            p.wait("tensor", "act", sq_done)
            start, stop = dc == 0, dc == DC - 1
            p.run("tensor", lambda e, xs=xs, s1=s1, start=start, stop=stop: e.matmul(
                s1, b.ones_col[:, 0:1], xs, start=start, stop=stop), inc=None)
            mm_s2[dc] = p.run("tensor", lambda e, t=sq_sl, s2=s2, start=start, stop=stop: e.matmul(
                s2, b.ones_col[:, 0:1], t, start=start, stop=stop), inc="mm")
        stats_done[th] = mm_s2[DC - 1]

    def emit_rows(th):
        s1 = ps[4 + th][0:1, :]
        s2 = ps[6 + th][0:1, :]
        ro = th * 2560
        R_MEAN = b.rows[0:1, ro + 0:ro + 512]
        R_MSQ = b.rows[0:1, ro + 512:ro + 1024]
        R_VAR = b.rows[0:1, ro + 1024:ro + 1536]
        R_RSTD = b.rows[0:1, ro + 1536:ro + 2048]
        R_B = b.rows[0:1, ro + 2048:ro + 2560]
        p.wait("vector", "mm", stats_done[th])
        p.runf("vector", lambda e, s1=s1, R=R_MEAN: e.tensor_scalar(R, s1, 1.0 / D, None, ALU.mult))
        p.runf("vector", lambda e, s2=s2, R=R_MSQ: e.tensor_scalar(R, s2, 1.0 / D, None, ALU.mult))
        p.runf("vector", lambda e, RM=R_MEAN, RV=R_VAR: e.scalar_tensor_tensor(
            RV, RM, -1.0, RM, ALU.mult, ALU.mult))
        c = p.runf("vector", lambda e, RV=R_VAR, RM=R_MSQ: e.tensor_tensor(RV, RV, RM, ALU.add))
        p.wait("scalar", "dve", c)
        p.runf("scalar", lambda e, RV=R_VAR: e.activation(RV, RV, AF.Ln, bias=np.float32(LN_EPS)))
        c = p.run("scalar", lambda e, RV=R_VAR, RR=R_RSTD: e.activation(
            RR, RV, AF.Exp, scale=np.float32(-0.5)), inc="act")
        p.wait("vector", "act", c)
        c = p.runf("vector", lambda e, RM=R_MEAN, RR=R_RSTD, RB=R_B: e.scalar_tensor_tensor(
            RB, RM, -1.0, RR, ALU.mult, ALU.mult))
        return c, R_RSTD, R_B

    def emit_bc(th, c, R_RSTD, R_B):
        bcA = ps[th * 2][:, :]
        bcB = ps[th * 2 + 1][:, :]
        p.wait("tensor", "dve", c)
        p.run("tensor", lambda e, bcA=bcA, RR=R_RSTD: e.matmul(
            bcA, b.ones_row[0:1, :], RR, start=True, stop=True), inc=None)
        mm_bc = p.run("tensor", lambda e, bcB=bcB, RB=R_B: e.matmul(
            bcB, b.ones_row[0:1, :], RB, start=True, stop=True), inc="mm")
        p.wait("vector", "mm", mm_bc)
        p.run("vector", lambda e, th=th, bcA=bcA: e.tensor_copy(
            bc[:, (th * 2) * 512:(th * 2) * 512 + 512], bcA), inc="dve")
        return p.run("vector", lambda e, th=th, bcB=bcB: e.tensor_copy(
            bc[:, (th * 2 + 1) * 512:(th * 2 + 1) * 512 + 512], bcB), inc="dve")

    def emit_applies():
        nonlocal last_apply
        evs = {}
        if order == "dc":
            pairs = [(dc, thi) for dc in range(DC) for thi in range(len(th_list))]
        else:
            pairs = [(dc, thi) for thi in range(len(th_list)) for dc in range(DC)]
        for idx, (dc, thi) in enumerate(pairs):
            th = th_list[thi]
            if True:
                if xw is not None and th == 1 and dc == 0:
                    p.wait("vector", xw[0], xw[1])
                t0 = th * 512
                bcA = bc[:, (th * 2) * 512:(th * 2) * 512 + 512]
                bcB = bc[:, (th * 2 + 1) * 512:(th * 2 + 1) * 512 + 512]
                xs = b.x[:, dc * TPC + t0: dc * TPC + t0 + 512]
                asl = a_dst[:, dc * TPC + t0: dc * TPC + t0 + 512]
                sl = idx % 2
                tsl = b.tmp[:, sl * 512:sl * 512 + 512]
                p.runf("vector", lambda e, xs=xs, t=tsl, bcA=bcA: e.tensor_tensor(t, xs, bcA, ALU.mult))
                last_apply = p.run("vector", lambda e, asl=asl, t=tsl, bcB=bcB: e.tensor_tensor(
                    asl, t, bcB, ALU.add), inc="dve")
                evs[(dc, th)] = last_apply
        return evs

    for th in th_list:
        emit_stats(th)
    rowres = {}
    for th in th_list:
        rowres[th] = emit_rows(th)
    for th in th_list:
        emit_bc(th, *rowres[th])
    evs = emit_applies()
    if want_ev:
        return last_apply, evs
    return last_apply


def plan_ffn(p, b, sb, a_bf, dram, first_call, a_ready_dve, between_th=None,
             out_stream=None, a_ev=None):
    nc = p.nc
    wgu = sb["wgu"]
    NB = TPC // 512
    pname = "1" if first_call else "2"

    def wsl(wpar, mat, dc):
        base = ((wpar * 2) + mat) * D
        return wgu[:, base + dc * 128: base + dc * 128 + 128]

    if a_ev is None:
        p.wait("tensor", "dve", a_ready_dve)
    if not first_call:
        p.wait("gpsimd", "mmo", p.val("mmo"))
    with ExitStack() as _ps:
        ps_g = [[_ps.enter_context(nc.psum_tensor("psg%d%d" % (par, nb) + pname, [128, 512], F32))
                 for nb in range(2)] for par in range(2)]
        ps_u = [[_ps.enter_context(nc.psum_tensor("psu%d%d" % (par, nb) + pname, [128, 512], F32))
                 for nb in range(2)] for par in range(2)]
        mm_done = [None] * HCN
        ev_done = [None] * HCN
        st_done = [None] * HCN
        if a_ev is None:
            p.wait("tensor", "act", p.val("act"))
        for h in range(HCN):
            par = h % 2
            wpar = h % 4
            if h >= 4:
                p.wait("gpsimd", "mmg", mm_done[h - 4])
            w_cnt = None
            for mat, wname in enumerate(("wgT", "wuT")):
                wt = dram[wname]
                src = wt[h * 128:(h + 1) * 128, :]
                dst = wgu[:, ((wpar * 2) + mat) * D: ((wpar * 2) + mat) * D + D]
                w_cnt = p.dma("gpsimd", dst, src, "w%d" % wpar + pname)
            if first_call:
                hc = h
                pp = hc % 2
                prep = sb["prep"]
                if hc >= 2:
                    p.wait("gpsimd", "prep_st%d" % pp, 64 * (hc // 2))
                ld = p.dma("gpsimd", prep[:, pp * D:(pp + 1) * D],
                           dram["wdT"][hc * 128:(hc + 1) * 128, :], "prep_ld%d" % pp)
                p.wait("sync", "prep_ld%d" % pp, ld)
                g8i, sl = hc // 8, hc % 8
                for qq in range(4):
                    p.dma("sync", dram["wd_bf"][(qq * 8 + g8i) * 128:(qq * 8 + g8i + 1) * 128,
                                                sl * 512:(sl + 1) * 512],
                          prep[:, pp * D + qq * 512: pp * D + (qq + 1) * 512], "prep_st%d" % pp)
            p.wait("tensor", "w%d" % wpar + pname, w_cnt)
            if h >= 2:
                p.wait("tensor", "dve", ev_done[h - 2])
            for mat in range(2):
                pst = (ps_g if mat == 0 else ps_u)[par]
                for dc in range(DC):
                    for nb in range(NB):
                        if a_ev is not None and h == 0 and mat == 0:
                            p.wait("tensor", "dve", a_ev[(dc, nb)])
                        start, stop = dc == 0, dc == DC - 1
                        last = mat == 1 and nb == NB - 1 and dc == DC - 1
                        f = lambda e, pst=pst, nb=nb, wpar=wpar, mat=mat, dc=dc, start=start, stop=stop: e.matmul(
                            pst[nb][:, :], wsl(wpar, mat, dc),
                            a_bf[:, dc * TPC + nb * 512: dc * TPC + (nb + 1) * 512],
                            start=start, stop=stop)
                        if last:
                            mm_done[h] = p.run("tensor", f, inc="mmg")
                        else:
                            p.run("tensor", f, inc=None)
            p.wait("scalar", "mmg", mm_done[h])
            if h >= 2:
                p.wait("scalar", "dve", ev_done[h - 2])
            for nb in range(NB):
                c = p.run("scalar", lambda e, par=par, nb=nb: e.activation(
                    sb["silu"][:, par * TPC + nb * 512: par * TPC + (nb + 1) * 512],
                    ps_g[par][nb][:, :], AF.Silu), inc="act")
            p.wait("vector", "act", c)
            if h >= 2:
                p.wait("vector", "g_st%d" % par + pname, st_done[h - 2])
            for nb in range(NB):
                ev_done[h] = p.run("vector", lambda e, par=par, nb=nb: e.scalar_tensor_tensor(
                    sb["gated"][:, par * TPC + nb * 512: par * TPC + (nb + 1) * 512],
                    ps_u[par][nb][:, :], 0.5,
                    sb["silu"][:, par * TPC + nb * 512: par * TPC + (nb + 1) * 512],
                    ALU.mult, ALU.mult), inc="dve")
            p.wait("sync", "dve", ev_done[h])
            g4i, sl4 = h // 4, h % 4
            st_done[h] = p.dma("sync", dram["gated"][g4i * 128:(g4i + 1) * 128,
                                                     sl4 * TPC:(sl4 + 1) * TPC],
                               sb["gated"][:, par * TPC:(par + 1) * TPC], "g_st%d" % par + pname)
        gu_last_ev = ev_done[HCN - 1]
    gst_tot = (p.val("g_st0" + pname), p.val("g_st1" + pname))

    wd_ready = (p.val("prep_st0"), p.val("prep_st1"))
    p.wait("tensor", "dve", gu_last_ev)
    gst_tot = (p.val("g_st0" + pname), p.val("g_st1" + pname))
    mmd_base = p.val("mmd")
    bank_guard = {}
    split = between_th is not None
    groups = [((0,), q) for q in range(4)] + [((1,), q) for q in range(4)] if split \
        else [((0, 1), q) for q in range(4)]
    with ExitStack() as _ps:
        ps_d = [_ps.enter_context(nc.psum_tensor("psd%d" % i + pname, [128, 512], F32))
                for i in range(8)]
        wd_in = sb["wd_in"]
        g_in = sb["g_in"]
        mmd_per_hc = 4 * (1 if split else 2) // 4
        for gi, (ths, q) in enumerate(groups):
            nth = len(ths)
            if split:
                bs = (gi % 2) * 4 if gi < 4 else 0
            else:
                bs = 0
            wldc = [None] * 8
            mm_hc = [None] * HCN
            for g8 in range(8):
                li = gi * 8 + g8
                if gi == 0 and g8 == 0 and first_call:
                    p.wait("scalar", "prep_st0", wd_ready[0])
                    p.wait("scalar", "prep_st1", wd_ready[1])
                prev = li - 2
                if prev >= 0:
                    p.wait("scalar", "mmd", mmd_base + (prev // 8) * HCN + (prev % 8) * 8 + 8)
                srcw = dram["wd_bf"][(q * 8 + g8) * 128:(q * 8 + g8 + 1) * 128, :]
                dstw = wd_in[:, (li % 2) * 4096:(li % 2) * 4096 + 4096]
                wldc[g8] = p.dma("scalar", dstw, srcw, "wd%d" % (li % 2) + pname)
            gldc = [None] * 16
            for g4 in range(16):
                lj = gi * 16 + g4
                if gi == 0 and g4 == 0:
                    p.wait("sync", "g_st0" + pname, gst_tot[0])
                    p.wait("sync", "g_st1" + pname, gst_tot[1])
                prev = lj - 2
                if prev >= 0:
                    p.wait("sync", "mmd", mmd_base + (prev // 16) * HCN + (prev % 16) * 4 + 4)
                if split:
                    th = ths[0]
                    srcg = dram["gated"][g4 * 128:(g4 + 1) * 128, :].rearrange(
                        "p (s t) -> p s t", t=TPC)[:, :, th * 512:(th + 1) * 512]
                    dstg = g_in[:, (lj % 2) * 2048:(lj % 2) * 2048 + 2048].rearrange(
                        "p (s t) -> p s t", t=512)
                else:
                    srcg = dram["gated"][g4 * 128:(g4 + 1) * 128, :]
                    dstg = g_in[:, (lj % 2) * 4096:(lj % 2) * 4096 + 4096]
                gldc[g4] = p.dma("sync", dstg, srcg, "gl%d" % (lj % 2) + pname)
            for hc in range(HCN):
                li = gi * 8 + hc // 8
                lj = gi * 16 + hc // 4
                if hc % 8 == 0:
                    p.wait("tensor", "wd%d" % (li % 2) + pname, wldc[hc // 8])
                if hc % 4 == 0:
                    p.wait("tensor", "gl%d" % (lj % 2) + pname, gldc[hc // 4])
                if hc == 0:
                    p.wait("tensor", "dve", bank_guard.get(bs, 0))
                gw = (lj % 2) * (2048 if split else 4096)
                for dt in range(4):
                    for thi, th in enumerate(ths):
                        start, stop = hc == 0, hc == HCN - 1
                        lastmm = dt == 3 and thi == nth - 1
                        if split:
                            rh = g_in[:, gw + (hc % 4) * 512: gw + (hc % 4) * 512 + 512]
                        else:
                            rh = g_in[:, gw + (hc % 4) * 1024 + th * 512:
                                      gw + (hc % 4) * 1024 + th * 512 + 512]
                        f = lambda e, bs=bs, li=li, hc=hc, dt=dt, thi=thi, nth=nth, rh=rh, start=start, stop=stop: e.matmul(
                            ps_d[bs + dt * nth + thi][:, :],
                            wd_in[:, (li % 2) * 4096 + (hc % 8) * 512 + dt * 128:
                                  (li % 2) * 4096 + (hc % 8) * 512 + dt * 128 + 128],
                            rh, start=start, stop=stop)
                        if lastmm:
                            mm_hc[hc] = p.run("tensor", f, inc="mmd")
                        else:
                            p.run("tensor", f, inc=None)
            p.wait("vector", "mmd", mm_hc[HCN - 1])
            for dt in range(4):
                for thi, th in enumerate(ths):
                    xsl = b.x[:, (q * 4 + dt) * TPC + th * 512:(q * 4 + dt) * TPC + (th + 1) * 512]
                    p.run("vector", lambda e, xsl=xsl, bs=bs, dt=dt, thi=thi, nth=nth: e.tensor_tensor(
                        xsl, ps_d[bs + dt * nth + thi][:, :], xsl, ALU.add), inc="dve")
            bank_guard[bs] = p.val("dve")
            if out_stream is not None:
                ev = p.val("dve")
                osrc = b.x[:, q * 4 * TPC:(q + 1) * 4 * TPC].rearrange(
                    "p (dc t) -> p dc t", t=TPC)
                odst = out_stream[:, :].rearrange(
                    "(dc p) t -> p dc t", p=128)[:, q * 4:(q + 1) * 4, :]
                if gi < len(groups) - 1:
                    p.wait("gpsimd", "dve", ev)
                    p.dma("gpsimd", odst, osrc, "s_out")
                else:
                    for hf, eng in enumerate(("gpsimd", "sync")):
                        p.wait(eng, "dve", ev)
                        p.dma(eng, odst[:, hf * 2:(hf + 1) * 2, :],
                              osrc[:, hf * 2:(hf + 1) * 2, :], "s_out")
            if split and gi == 3 and between_th is not None:
                between_th(ps_d)()
    return p.val("dve")


def plan_store_a(p, a_bf, dst_dram, th, eng, ev):
    p.wait(eng, "dve", ev)
    src = a_bf[:, :].rearrange("p (dc t) -> p dc t", t=TPC)[:, :, th * 512:(th + 1) * 512]
    dst = dst_dram[:, :].rearrange("p (dc c) -> p dc c", c=512)
    return p.dma(eng, dst, src, "a_st%d" % th)


def plan_collective(p, kind, ins, outs, sem_name, wait_dma=None):
    if wait_dma is not None:
        p.wait("gpsimd", wait_dma[0], wait_dma[1])
    s = p.sem(sem_name)
    p.cnt[sem_name] += 1
    p.rec.setdefault("gpsimd", []).append(("inc", sem_name, 1))

    def f(e, kind=kind, ins=ins, outs=outs, s=s):
        if kind == "AllToAll":
            with allow_ici_all_to_all():
                return e.collective_compute(kind, ALU.bypass, replica_groups=[list(range(NCORES))],
                                            ins=ins, outs=outs).then_inc(s)
        return e.collective_compute(kind, ALU.bypass, replica_groups=[list(range(NCORES))],
                                    ins=ins, outs=outs).then_inc(s)
    p.on("gpsimd", f)
    return p.cnt[sem_name]


def plan_attn_prep(p, b, at, dram):
    LB = 4608
    Lr = lambda a, z: b.rows[0:1, LB + a:LB + z]
    c_lam = p.dma("sync", Lr(0, 512), dram["lam_row"][:, :], "prep2")
    c_gn = p.dma("sync", at["gbpad"][0:4, :], dram["gn4"][:, :], "prep2")
    c_msk = p.dma("sync", at["masks"][:, :], dram["consts_bf"][:, :], "prep2")
    c_idf = p.dma("sync", at["identf"][:, :], dram["ident_f32"][:, :], "prep2")
    p.wait("vector", "prep2", c_idf)
    c = p.runf("vector", lambda e: e.tensor_tensor(Lr(512, 640), Lr(0, 128), Lr(128, 256), ALU.mult))
    c = p.runf("vector", lambda e: e.tensor_tensor(Lr(640, 768), Lr(256, 384), Lr(384, 512), ALU.mult))
    p.wait("scalar", "dve", c)
    c = p.run("scalar", lambda e: e.activation(Lr(0, 256), Lr(512, 768), AF.Exp), inc="act")
    p.wait("vector", "act", c)
    c = p.runf("vector", lambda e: e.tensor_tensor(Lr(256, 384), Lr(0, 128), Lr(128, 256), ALU.subtract))
    lam_done = p.runf("vector", lambda e: e.tensor_scalar(
        Lr(256, 384), Lr(256, 384), LAMBDA_INIT, None, ALU.add))
    at["lam_row128"] = Lr(256, 384)
    p.wait("tensor", "prep2", c_idf)
    c = p.run("tensor", lambda e: e.transpose(at["ps_sp"][:, 0:128], at["gbpad"][:, :], at["identf"][:, :]), inc="mA")
    p.wait("tensor", "dve", lam_done)
    c2 = p.run("tensor", lambda e: e.transpose(at["ps_sp"][:, 128:129], at["lam_row128"],
                                               at["identf"][0:1, 0:1]), inc="mA")
    p.wait("vector", "mA", c2)
    p.runf("vector", lambda e: e.tensor_scalar(
        at["gb"][:, :], at["ps_sp"][:, 0:4], 1.0 - LAMBDA_INIT, None, ALU.mult))
    p.runf("vector", lambda e: e.tensor_copy(at["lamc"][:, :], at["ps_sp"][:, 128:129]))
    return c_idf


def plan_qkv_head(p, b, at, hl, dram, psq, ag1_done):
    p.wait("gpsimd", "mq", p.val("mq"))
    src = dram["wqkvT"][hl * 128:(hl + 1) * 128, :]
    dst = at["wq"][:, :]
    wq_cnt = p.dma("gpsimd", dst, src, "wqkv")
    p.wait("sync", "a_st0", dram.get("_a_st_a"))
    p.wait("sync", "a_st1", dram.get("_a_st_b"))
    p.wait("sync", "mq", p.val("mq"))
    p.wait("tensor", "prep2", p.val("prep2"))
    p.wait("vector", "h_st0", p.val("h_st0"))
    p.wait("vector", "h_st1", p.val("h_st1"))
    ev_v = [None] * 16
    tp_done = [None] * 16
    mq_done = [None] * 16
    order = [2 * i for i in range(8)] + [2 * i + 1 for i in range(8)]
    for oi, tc in enumerate(order):
        pi = oi % 2
        r, half = tc // 2, tc % 2
        if oi >= 2:
            p.wait("sync", "mq", mq_done[order[oi - 2]])
        if oi == 0:
            p.wait("sync", "cc1a", ag1_done[0])
        if oi == 8:
            p.wait("sync", "cc1b", ag1_done[1])
        srcn = dram["ag1_out_" + ("a" if half == 0 else "b")]
        src = srcn[r * 128:(r + 1) * 128, :].rearrange("p (dc c) -> p dc c", c=512)
        dst = at["chunk"][:, pi * 8192:(pi + 1) * 8192].rearrange("p (dc t) -> p dc t", t=512)
        ld = p.dma("sync", dst, src, "aq%d" % pi)
        p.wait("tensor", "aq%d" % pi, ld)
        if oi == 0:
            p.wait("tensor", "wqkv", wq_cnt)
        if oi >= 2:
            p.wait("tensor", "dve", ev_v[order[oi - 2]])
            p.wait("tensor", "act", p.val("act"))
        mq = None
        for mat in range(3):
            T = psq["T"][mat][pi]
            for dc in range(DC):
                start, stop = dc == 0, dc == DC - 1
                f = lambda e, T=T, pi=pi, mat=mat, dc=dc, start=start, stop=stop: e.matmul(
                    T[:, :],
                    at["wq"][:, dc * 384 + mat * 128: dc * 384 + mat * 128 + 128],
                    at["chunk"][:, pi * 8192 + dc * 512: pi * 8192 + (dc + 1) * 512],
                    start=start, stop=stop)
                last = mat == 2 and dc == DC - 1
                if last:
                    mq = p.run("tensor", f, inc="mq")
                else:
                    p.run("tensor", f, inc=None)
        mq_done[tc] = mq
        p.wait("vector", "mq", mq)
        p.run("vector", lambda e, tc=tc, pi=pi: e.tensor_copy(
            at["q_fm"][:, tc * 512:(tc + 1) * 512], psq["T"][0][pi][:, :]), inc=None)
        if oi >= 2:
            p.wait("vector", "mq", tp_done[order[oi - 2]][0])
        ev_v[tc] = p.run("vector", lambda e, tc=tc, pi=pi: e.tensor_copy(
            at["v_st"][:, pi * 512:(pi + 1) * 512], psq["T"][2][pi][:, :]), inc="dve")
        p.wait("scalar", "mq", mq)
        p.run("scalar", lambda e, tc=tc, pi=pi: e.activation(
            at["k_fm"][:, tc * 512:(tc + 1) * 512], psq["T"][1][pi][:, :], AF.Copy,
            bias=0.0), inc="act")
        p.wait("tensor", "dve", ev_v[tc])
        if oi >= 2:
            p.wait("tensor", "dve", tp_done[order[oi - 2]][1])
        tmm = None
        for i in range(4):
            tmm = p.run("tensor", lambda e, pi=pi, i=i: e.transpose(
                psq["ps_t"][pi][:, i * 128:(i + 1) * 128],
                at["v_st"][:, pi * 512 + i * 128: pi * 512 + (i + 1) * 128],
                at["ident_bf"][:, :]), inc="mq")
        p.wait("vector", "mq", tmm)
        tcopy = p.run("vector", lambda e, tc=tc, pi=pi: e.tensor_copy(
            at["v_tm"][:, tc * 512:(tc + 1) * 512], psq["ps_t"][pi][:, 0:512]), inc="dve")
        tp_done[tc] = (tmm, tcopy)
    return tp_done[order[-1]][1]


def plan_attn_head(p, b, at, hl, dram, psa, qkv_done, dbg=None):
    sA, sB, aC, dD = psa
    scale = np.float32(0.125)
    p.wait("tensor", "dve", qkv_done)
    p.wait("tensor", "act", p.val("act"))
    p.wait("scalar", "mA", p.val("mA"))
    p.wait("scalar", "dve", p.val("dve"))
    p.wait("vector", "mA", p.val("mA"))
    e2 = {}
    m2 = {}
    s_cnt = {}
    av_cnt = {}
    acc_cnt = {}
    gof = [0]
    st_prev = [None, None]
    ev_state = {"a1": None, "a2": None}
    NEB = 6

    def emit_scores(j, kv, diag_m):
        g = gof[0] + kv
        sp = kv % 2
        qsl = slice(j * QB, (j + 1) * QB)
        if g >= 2 and (g - 2) in e2:
            p.wait("tensor", "act", e2[g - 2])
        for br in range(2):
            ps = (sA if br == 0 else sB)[sp][:, :]
            lh = at["k_fm"][br * 64:(br + 1) * 64, kv * 128:(kv + 1) * 128]
            rh = at["q_fm"][br * 64:(br + 1) * 64, qsl]
            f = lambda e, ps=ps, lh=lh, rh=rh, stop=diag_m is None: e.matmul(
                ps, lh, rh, start=True, stop=stop)
            if diag_m is None and br == 1:
                s_cnt[g] = p.run("tensor", f, inc="mA")
            else:
                p.run("tensor", f, inc=None)
            if diag_m is not None:
                fm = lambda e, ps=ps, m=diag_m: e.matmul(
                    ps, at["tri_bf"][:, :],
                    at["masks"][:, m * 512:(m + 1) * 512], start=False, stop=True)
                if br == 1:
                    s_cnt[g] = p.run("tensor", fm, inc="mA")
                else:
                    p.run("tensor", fm, inc=None)

    def emit_exp(j, kv, diag_m):
        g = gof[0] + kv
        sp = kv % 2
        bi = g % NEB
        p.wait("scalar", "mA", s_cnt[g])
        if g >= NEB and (g - NEB) in av_cnt:
            p.wait("scalar", "mA", av_cnt[g - NEB])
            p.wait("scalar", "dve", acc_cnt[g - NEB])
        p.run("scalar", lambda e, sp=sp, bi=bi: e.activation(
            at["exp"][:, bi * 512:bi * 512 + 512],
            sA[sp][:, :], AF.Exp, scale=scale), inc="act")
        e2[g] = p.run("scalar", lambda e, sp=sp, bi=bi: e.activation(
            at["exp"][:, (NEB + bi) * 512:(NEB + bi) * 512 + 512],
            sB[sp][:, :], AF.Exp, scale=scale), inc="act")
        p.wait("vector", "act", e2[g])
        rv = (lambda eng, f: p.runf(eng, f)) if FLUSH_HOT else (lambda eng, f: p.run(eng, f, inc="dve"))
        for br in range(2):
            esl = at["exp"][:, (br * NEB + bi) * 512:(br * NEB + bi) * 512 + 512]
            accsl = at["acc"][:, br * 512:(br + 1) * 512]
            if kv == 0:
                p.wait("vector", "dve", ev_state.get("c2"))
                c = rv("vector", lambda e, accsl=accsl, esl=esl: e.tensor_copy(accsl, esl))
            else:
                c = rv("vector", lambda e, accsl=accsl, esl=esl: e.tensor_tensor(
                    accsl, accsl, esl, ALU.add))
            if br == 1:
                acc_cnt[g] = c

    def emit_av(j, kv, diag, last):
        g = gof[0] + kv
        bi = g % NEB
        p.wait("tensor", "act", e2[g])
        if kv == 0:
            p.wait("tensor", "dve", ev_state["a2"])
        for br in range(2):
            ps = aC[br][:, :]
            lh = at["v_tm"][:, kv * 128:(kv + 1) * 128]
            rh = at["exp"][:, (br * NEB + bi) * 512:(br * NEB + bi) * 512 + 512]
            f = lambda e, ps=ps, lh=lh, rh=rh, start=kv == 0, stop=last: e.matmul(
                ps, lh, rh, start=start, stop=stop)
            if br == 1:
                av_cnt[g] = p.run("tensor", f, inc="mA")
            else:
                p.run("tensor", f, inc=None)

    def make_epilogue(j):
        A1s = b.tmp[:, 0:512]
        A2s = b.tmp[:, 512:1024]
        rho_b = b.sq[:, 0:512]
        osq = b.sq[:, 512:1024]
        R1 = b.rows[0:1, 0:512]
        R2 = b.rows[0:1, 512:1024]
        R3 = b.rows[0:1, 1024:1536]
        R4 = b.rows[0:1, 1536:2048]
        R5 = b.rows[0:1, 2048:2560]
        R6 = b.rows[0:1, 2560:3072]
        R7 = b.rows[0:1, 3072:3584]
        st = {}

        av_last = av_cnt[gof[0] - 1]
        p.wait("vector", "mA", av_last)
        ev_state["a1"] = None
        c = p.runf("vector", lambda e: e.tensor_copy(A1s, aC[0][:, :]))
        ev_state["a1"] = c
        ev_state["a2"] = p.runf("vector", lambda e: e.tensor_copy(A2s, aC[1][:, :]))
        p.run("vector", lambda e: e.tensor_copy(at["accbf"][:, 0:512], at["acc"][:, 0:512]), inc=None)
        st["c2"] = p.run("vector", lambda e: e.tensor_copy(at["accbf"][:, 512:1024], at["acc"][:, 512:1024]), inc="dve")
        ev_state["c2"] = st["c2"]

        def round1():
            p.wait("tensor", "dve", st["c2"])
            p.run("tensor", lambda e: e.matmul(dD[0][0:1, :], b.ones_col_bf[:, 0:1],
                                               at["accbf"][:, 0:512], start=True, stop=True), inc=None)
            cd = p.run("tensor", lambda e: e.matmul(dD[1][0:1, :], b.ones_col_bf[:, 0:1],
                                                    at["accbf"][:, 512:1024], start=True, stop=True), inc="mA")
            p.wait("scalar", "mA", cd)
            p.runf("scalar", lambda e: e.activation(R1, dD[0][0:1, :], AF.Ln))
            c = p.run("scalar", lambda e: e.activation(R2, dD[1][0:1, :], AF.Ln), inc="act")
            p.wait("vector", "act", c)
            c = p.runf("vector", lambda e: e.tensor_tensor(R2, R1, R2, ALU.subtract))
            p.wait("scalar", "dve", c)
            st["rho"] = p.run("scalar", lambda e: e.activation(R2, R2, AF.Exp), inc="act")

        def round2():
            p.wait("tensor", "act", st["rho"])
            crb = p.run("tensor", lambda e: e.matmul(dD[0][:, :], b.ones_row[0:1, :], R2,
                                                     start=True, stop=True), inc="mA")
            p.wait("vector", "mA", crb)
            p.runf("vector", lambda e: e.tensor_copy(rho_b, dD[0][:, :]))
            p.runf("vector", lambda e: e.tensor_tensor(A2s, A2s, rho_b, ALU.mult))
            p.runf("vector", lambda e: e.tensor_scalar(A2s, A2s, at["lamc"][:, 0:1], None, ALU.mult))
            p.runf("vector", lambda e: e.tensor_tensor(A1s, A1s, A2s, ALU.subtract))
            st["o"] = p.runf("vector", lambda e: e.tensor_tensor(osq, A1s, A1s, ALU.mult))

        def round3():
            p.wait("tensor", "dve", st["o"])
            p.run("tensor", lambda e: e.matmul(dD[0][0:1, :], b.ones_col[:, 0:1], A1s,
                                               start=True, stop=True), inc=None)
            cs = p.run("tensor", lambda e: e.matmul(dD[1][0:1, :], b.ones_col[:, 0:1], osq,
                                                    start=True, stop=True), inc="mA")
            p.wait("vector", "mA", cs)
            p.runf("vector", lambda e: e.tensor_scalar(R3, dD[0][0:1, :], 1.0 / 128.0, None, ALU.mult))
            p.runf("vector", lambda e: e.tensor_scalar(R4, dD[1][0:1, :], 1.0 / 128.0, None, ALU.mult))
            p.runf("vector", lambda e: e.scalar_tensor_tensor(R5, R3, -1.0, R3, ALU.mult, ALU.mult))
            cv = p.runf("vector", lambda e: e.tensor_tensor(R5, R5, R4, ALU.add))
            p.wait("scalar", "dve", cv)
            p.runf("scalar", lambda e: e.activation(R6, R5, AF.Ln, bias=np.float32(LN_EPS)))
            cq = p.run("scalar", lambda e: e.activation(R6, R6, AF.Exp, scale=np.float32(-0.5)), inc="act")
            p.wait("vector", "act", cq)
            st["gb"] = p.runf("vector", lambda e: e.scalar_tensor_tensor(R7, R3, -1.0, R6, ALU.mult, ALU.mult))

        def round4():
            p.wait("tensor", "dve", st["gb"])
            p.run("tensor", lambda e: e.matmul(dD[0][:, :], b.ones_row[0:1, :], R6,
                                               start=True, stop=True), inc=None)
            cab = p.run("tensor", lambda e: e.matmul(dD[1][:, :], b.ones_row[0:1, :], R7,
                                                     start=True, stop=True), inc="mA")
            p.wait("vector", "mA", cab)
            p.runf("vector", lambda e: e.tensor_tensor(at["on"][:, 0:512], A1s, dD[0][:, :], ALU.mult))
            p.runf("vector", lambda e: e.tensor_tensor(at["on"][:, 0:512], at["on"][:, 0:512], dD[1][:, :], ALU.add))
            pj = j % 2
            if st_prev[pj] is not None:
                p.wait("vector", "h_st%d" % pj, st_prev[pj])
            cfin = p.run("vector", lambda e, pj=pj: e.tensor_scalar(
                at["head_bf"][:, pj * 512:(pj + 1) * 512], at["on"][:, 0:512],
                at["gb"][:, hl * 2:hl * 2 + 1], at["gb"][:, hl * 2 + 1:hl * 2 + 2],
                ALU.mult, ALU.add), inc="dve")
            p.wait("sync", "dve", cfin)
            st_prev[pj] = p.dma("sync", dram["a2a_in_" + ("a" if hl == 0 else "b")][
                                (j // 2) * 128:(j // 2) * 128 + 128,
                                (j % 2) * 512:(j % 2) * 512 + 512],
                                at["head_bf"][:, pj * 512:(pj + 1) * 512], "h_st%d" % pj)
        return [round1, round2, round3, round4]

    pend = []
    for j in range(NQB):
        ktiles = 4 * (j + 1)
        for kv in range(ktiles):
            diag_m = kv - 4 * j if kv >= 4 * j else None
            emit_scores(j, kv, diag_m)
            emit_exp(j, kv, diag_m)
            if kv >= 3:
                emit_av(j, kv - 3, (kv - 3) >= 4 * j, False)
            if pend and kv >= 1:
                pend.pop(0)()
        for t in (3, 2):
            emit_av(j, ktiles - t, (ktiles - t) >= 4 * j, False)
        emit_av(j, ktiles - 1, True, True)
        while pend:
            pend.pop(0)()
        gof[0] += ktiles
        pend = make_epilogue(j)
    while pend:
        pend.pop(0)()
    return (p.val("h_st0"), p.val("h_st1"))



def plan_wo(p, b, dram, a2a_done):
    nc = p.nc
    with nc.sbuf_tensor("wo_rhs", [128, DC * TPC], BF16) as wo_rhs, \
         nc.sbuf_tensor("wo_w", [128, 2 * D], BF16) as wo_w, \
         nc.psum_tensor("pso0", [128, 1024], F32) as o0, \
         nc.psum_tensor("pso1", [128, 1024], F32) as o1:
        pso = [o0, o1]
        dst3 = wo_rhs[:, :].rearrange("p (hc t) -> p hc t", t=TPC)
        rhs_cnt = {}
        for hfi, (hf, eng) in enumerate((("a", "sync"), ("b", "scalar"))):
            p.wait(eng, "cc2" + hf, 1)
            src = dram["a2a_out_" + hf][:, :].rearrange("(s p) t -> p s t", p=128)
            rhs_cnt[hfi] = p.dma(eng, dst3[:, hfi::2, :], src, "wo_rhs_ld%d" % hfi)
        mmo = [None] * DC
        evo = [None] * DC
        for do in range(DC):
            par = do % 2
            if do >= 2:
                p.wait("gpsimd", "mmo", mmo[do - 2])
            src = dram["woT"][do * 128:(do + 1) * 128, :]
            dst = wo_w[:, par * D:(par + 1) * D]
            wcnt = p.dma("gpsimd", dst, src, "wo_w_ld%d" % par)
            p.wait("tensor", "wo_w_ld%d" % par, wcnt)
            if do == 0:
                p.wait("tensor", "wo_rhs_ld0", rhs_cnt[0])
                p.wait("tensor", "dve", p.val("dve"))
                p.wait("tensor", "act", p.val("act"))
            if do >= 2:
                p.wait("tensor", "dve", evo[do - 2])
            hc_order = list(range(0, DC, 2)) + list(range(1, DC, 2))
            for hci, hc in enumerate(hc_order):
                if do == 0 and hci == DC // 2:
                    p.wait("tensor", "wo_rhs_ld1", rhs_cnt[1])
                for nb in range(2):
                    start, stop = hci == 0, hci == DC - 1
                    last = nb == 1 and hci == DC - 1
                    f = lambda e, par=par, nb=nb, hc=hc, do=do, start=start, stop=stop: e.matmul(
                        pso[par][:, nb * 512:(nb + 1) * 512],
                        wo_w[:, par * D + hc * 128: par * D + hc * 128 + 128],
                        wo_rhs[:, hc * TPC + nb * 512: hc * TPC + (nb + 1) * 512],
                        start=start, stop=stop)
                    if last:
                        mmo[do] = p.run("tensor", f, inc="mmo")
                    else:
                        p.run("tensor", f, inc=None)
            p.wait("vector", "mmo", mmo[do])
            xsl = b.x[:, do * TPC:(do + 1) * TPC]
            evo[do] = p.run("vector", lambda e, xsl=xsl, par=par: e.tensor_tensor(
                xsl, pso[par][:, :], xsl, ALU.add), inc="dve")
    return p.val("dve")


def build_full(stop_after="full", skip_ffn1=False):
    nc = bass.Bass("TRN2", target_bir_lowering=False, debug=False)
    dram = {}
    x_fm = nc.dram_tensor("x_fm", [D, TPC], F32, kind="ExternalInput").ap()
    dram["wgT"] = nc.dram_tensor("wgT", [HCN * 128, D], F32, kind="ExternalInput").ap()
    dram["wuT"] = nc.dram_tensor("wuT", [HCN * 128, D], F32, kind="ExternalInput").ap()
    dram["wdT"] = nc.dram_tensor("wdT", [H, D], F32, kind="ExternalInput").ap()
    dram["wqkvT"] = nc.dram_tensor("wqkvT", [2 * 128, DC * 384], F32, kind="ExternalInput").ap()
    dram["woT"] = nc.dram_tensor("woT", [DC * 128, D], F32, kind="ExternalInput").ap()
    dram["lam_row"] = nc.dram_tensor("lam_row", [1, 512], F32, kind="ExternalInput").ap()
    dram["gn4"] = nc.dram_tensor("gn4", [4, 128], F32, kind="ExternalInput").ap()
    dram["consts_bf"] = nc.dram_tensor("consts_bf", [128, 4 * 512 + 256], BF16, kind="ExternalInput").ap()
    dram["ident_f32"] = nc.dram_tensor("ident_f32", [128, 128], F32, kind="ExternalInput").ap()
    out_fm = nc.dram_tensor("out_fm", [D, TPC], F32, kind="ExternalOutput").ap()
    dbg = nc.dram_tensor("dbg", [3 * 128, S], BF16, kind="ExternalOutput").ap()         if stop_after in ("qkv", "attn", "ag1") else None

    dram["gated"] = nc.dram_tensor("gated_scr", [16 * 128, 4 * TPC], BF16).ap()
    dram["wd_bf"] = nc.dram_tensor("wd_bf_scr", [32 * 128, 8 * 512], BF16).ap()
    kw = {"addr_space": "Shared"} if SHARED_AG else {}
    for hf in "ab":
        dram["ag1_in_" + hf] = nc.dram_tensor("ag1_in_" + hf, [128, DC * 512], BF16).ap()
        dram["ag1_out_" + hf] = nc.dram_tensor("ag1_out_" + hf, [NCORES * 128, DC * 512], BF16, **kw).ap()
    for hf in "ab":
        dram["a2a_in_" + hf] = nc.dram_tensor("a2a_in_" + hf, [NCORES * 128, TPC], BF16).ap()
        dram["a2a_out_" + hf] = nc.dram_tensor("a2a_out_" + hf, [NCORES * 128, TPC], BF16).ap()

    with ExitStack() as stack:
        p = Plan(nc, stack)
        b = alloc_persistent(p, stack)
        plan_init(p, b)
        xc = plan_load_x(p, b, x_fm)
        for e in ("scalar", "tensor", "vector"):
            p.wait(e, "s_x0", xc[0])

        with ExitStack() as f1:
            a_bf = f1.enter_context(nc.sbuf_tensor("a1_bf", [128, DC * TPC], BF16))
            sb = {
                "wgu": f1.enter_context(nc.sbuf_tensor("wgu1", [128, 4 * 2 * D], BF16)),
                "silu": f1.enter_context(nc.sbuf_tensor("silu1", [128, 2 * TPC], BF16)),
                "gated": f1.enter_context(nc.sbuf_tensor("gated1", [128, 2 * TPC], BF16)),
                "prep": f1.enter_context(nc.sbuf_tensor("prep1", [128, 2 * D], BF16)),
                "wd_in": f1.enter_context(nc.sbuf_tensor("wd_in1", [128, 2 * 8 * 512], BF16)),
                "g_in": f1.enter_context(nc.sbuf_tensor("g_in1", [128, 2 * 4 * 1024], BF16)),
            }
            with ExitStack() as _ps:
                lps = [_ps.enter_context(nc.psum_tensor("lnp%d" % i, [128, 512], F32)) for i in range(8)]
                bc1 = sb["g_in"][:, 0:4096].bitcast(F32)
                a_ready, a_ev1 = plan_ln(p, b, a_bf, lps, xw=("s_x1", xc[1]),
                                         want_ev=True, bc=bc1)
            if not skip_ffn1:
                plan_ffn(p, b, sb, a_bf, dram, first_call=True, a_ready_dve=a_ready,
                         a_ev=a_ev1)
            with ExitStack() as _ps:
                lps = [_ps.enter_context(nc.psum_tensor("lnq%d" % i, [128, 512], F32)) for i in range(8)]
                a2_ready, a_ev2 = plan_ln(p, b, a_bf, lps, want_ev=True, bc=bc1,
                                          order="th")
            a_st_a = plan_store_a(p, a_bf, dram["ag1_in_a"], 0, "sync", a_ev2[(DC - 1, 0)])
            ag1a = plan_collective(p, "AllGather", [dram["ag1_in_a"][:, :]],
                                   [dram["ag1_out_a"][:, :]], "cc1a",
                                   wait_dma=("a_st0", a_st_a))
            a_st_b = plan_store_a(p, a_bf, dram["ag1_in_b"], 1, "gpsimd", a2_ready)
            dram["_a_st_a"] = a_st_a
            dram["_a_st_b"] = a_st_b
            if stop_after == "ffn1":
                finish(p, b, out_fm)
                return nc
        ag1_done = (ag1a,
                    plan_collective(p, "AllGather", [dram["ag1_in_b"][:, :]],
                                    [dram["ag1_out_b"][:, :]], "cc1b",
                                    wait_dma=("a_st1", a_st_b)))

        with ExitStack() as f2:
            at = {
                "wq": f2.enter_context(nc.sbuf_tensor("wq", [128, DC * 384], BF16)),
                "chunk": f2.enter_context(nc.sbuf_tensor("chunk", [128, 2 * 8192], BF16)),
                "q_fm": f2.enter_context(nc.sbuf_tensor("q_fm", [128, S], BF16)),
                "k_fm": f2.enter_context(nc.sbuf_tensor("k_fm", [128, S], BF16)),
                "v_tm": f2.enter_context(nc.sbuf_tensor("v_tm", [128, S], BF16)),
                "v_st": f2.enter_context(nc.sbuf_tensor("v_st", [128, 1024], BF16)),
                "exp": f2.enter_context(nc.sbuf_tensor("exp_sb", [128, 12 * 512], BF16)),

                "masks": f2.enter_context(nc.sbuf_tensor("masks", [128, 4 * 512 + 256], BF16)),
                "identf": f2.enter_context(nc.sbuf_tensor("identf", [128, 128], F32)),
                "gbpad": f2.enter_context(nc.sbuf_tensor("gbpad", [128, 128], F32)),
                "gb": f2.enter_context(nc.sbuf_tensor("gb", [128, 4], F32)),
                "lamc": f2.enter_context(nc.sbuf_tensor("lamc", [128, 1], F32)),
            }
            at["ident_bf"] = at["masks"][:, 4 * 512:4 * 512 + 128]
            at["tri_bf"] = at["masks"][:, 4 * 512 + 128:4 * 512 + 256]
            at["head_bf"] = at["v_st"]
            at["o"] = at["chunk"][:, 0:2048].bitcast(F32)
            at["on"] = at["chunk"][:, 2048:4096].bitcast(F32)
            at["accbf"] = at["chunk"][:, 4096:5120]
            at["acc"] = at["chunk"][:, 5120:7168].bitcast(F32)
            for hl in range(2):
                if hl == 0:
                    with nc.psum_tensor("qs_prep", [128, 512], F32) as ps_sp:
                        at["ps_sp"] = ps_sp
                        c_msk = plan_attn_prep(p, b, at, dram)
                        p.wait("vector", "prep2", c_msk)
                with ExitStack() as _ps:
                    psq = {"T": [[_ps.enter_context(nc.psum_tensor("q%d%d_%d" % (m, pi, hl), [128, 512], F32))
                                  for pi in range(2)] for m in range(3)],
                           "ps_t": [_ps.enter_context(nc.psum_tensor("qt%d_%d" % (pi, hl), [128, 1024], BF16))
                                    for pi in range(2)]}
                    p.wait("tensor", "dve", p.val("dve"))
                    p.wait("tensor", "act", p.val("act"))
                    p.wait("tensor", "mA", p.val("mA"))
                    qkv_done = plan_qkv_head(p, b, at, hl, dram, psq, ag1_done)
                if stop_after == "qkv":
                    for i, t in enumerate(("q_fm", "k_fm", "v_tm")):
                        p.wait("sync", "dve", p.val("dve"))
                        p.wait("sync", "act", p.val("act"))
                        p.dma("sync", dbg[i * 128:(i + 1) * 128, :], at[t][:, :], "s_out")
                    finish(p, b, out_fm)
                    return nc
                with ExitStack() as _ps:
                    sA = [_ps.enter_context(nc.psum_tensor("aA%d_%d" % (i, hl), [128, 512], F32)) for i in range(2)]
                    sB = [_ps.enter_context(nc.psum_tensor("aB%d_%d" % (i, hl), [128, 512], F32)) for i in range(2)]
                    aC = [_ps.enter_context(nc.psum_tensor("aC%d_%d" % (i, hl), [128, 512], F32)) for i in range(2)]
                    dD = [_ps.enter_context(nc.psum_tensor("aD%d_%d" % (i, hl), [128, 512], F32)) for i in range(2)]
                    hst = plan_attn_head(p, b, at, hl, dram, (sA, sB, aC, dD), qkv_done, dbg=dbg)
                hf = "a" if hl == 0 else "b"
                p.wait("gpsimd", "h_st0", hst[0])
                p.wait("gpsimd", "h_st1", hst[1])
                plan_collective(p, "AllToAll", [dram["a2a_in_" + hf][:, :]],
                                [dram["a2a_out_" + hf][:, :]], "cc2" + hf, wait_dma=None)
        plan_wo(p, b, dram, None)
        if stop_after == "wo":
            finish(p, b, out_fm)
            return nc

        with ExitStack() as f3:
            a_bf = f3.enter_context(nc.sbuf_tensor("a3_bf", [128, DC * TPC], BF16))
            sb = {
                "wgu": f3.enter_context(nc.sbuf_tensor("wgu2", [128, 4 * 2 * D], BF16)),
                "silu": f3.enter_context(nc.sbuf_tensor("silu2", [128, 2 * TPC], BF16)),
                "gated": f3.enter_context(nc.sbuf_tensor("gated2", [128, 2 * TPC], BF16)),
                "prep": None,
                "wd_in": f3.enter_context(nc.sbuf_tensor("wd_in2", [128, 2 * 8 * 512], BF16)),
                "g_in": f3.enter_context(nc.sbuf_tensor("g_in2", [128, 2 * 4 * 1024], BF16)),
            }
            p.wait("scalar", "dve", p.val("dve"))
            p.wait("tensor", "dve", p.val("dve"))
            with ExitStack() as _ps:
                lps = [_ps.enter_context(nc.psum_tensor("lnr%d" % i, [128, 512], F32)) for i in range(8)]
                bc3 = sb["g_in"][:, 0:4096].bitcast(F32)
                a3_ready, a_ev3 = plan_ln(p, b, a_bf, lps, want_ev=True, bc=bc3)
            fin = plan_ffn(p, b, sb, a_bf, dram, first_call=False, a_ready_dve=a3_ready,
                           out_stream=out_fm, a_ev=a_ev3)

        finish(p, b, out_fm, streamed=True)
    return nc


def finish(p, b, out_fm, streamed=False):
    p.wait("sync", "dve", p.val("dve"))
    p.wait("sync", "act", p.val("act"))
    if not streamed:
        src = b.x[:, :].rearrange("p (dc t) -> p dc t", t=TPC)
        dst = out_fm[:, :].rearrange("(dc p) t -> p dc t", p=128)
        p.dma("sync", dst, src, "s_out")
    p.wait("sync", "s_out", p.val("s_out"))
    p.check_deadlock()
    p.emit()


def make_host_inputs(inp):
    x = np.asarray(inp["L0_attn_x"])[0]
    W_qkv = np.asarray(inp["L0_attn_W_qkv"])
    W_o = np.asarray(inp["L0_attn_W_o"])
    wg = np.asarray(inp["L0_ffn_w_gate"])
    wu = np.asarray(inp["L0_ffn_w_up"])
    wd = np.asarray(inp["L0_ffn_w_down"])
    lq1 = np.asarray(inp["L0_attn_lambda_q1"])
    lk1 = np.asarray(inp["L0_attn_lambda_k1"])
    lq2 = np.asarray(inp["L0_attn_lambda_q2"])
    lk2 = np.asarray(inp["L0_attn_lambda_k2"])
    gamma = np.asarray(inp["L0_attn_gn_gamma"])
    beta = np.asarray(inp["L0_attn_gn_beta"])

    def tile_w(w):
        ho = w.shape[0] // 128
        t = w.reshape(ho, 128, w.shape[1] // 128, 128).transpose(0, 3, 2, 1)
        return np.ascontiguousarray(t.reshape(ho * 128, w.shape[1]))
    wgT = tile_w(wg)
    wuT = tile_w(wu)
    wdT = np.ascontiguousarray(wd.T)
    woT = tile_w(W_o)
    lam_row = np.concatenate([lq1, lk1, lq2, lk2]).astype(np.float32).reshape(1, 512)
    masks = np.zeros((128, 4 * 512 + 256), dtype=ml_dtypes.bfloat16)
    r = np.arange(128)[:, None]
    c = np.arange(512)[None, :]
    for m in range(4):
        masks[:, m * 512:(m + 1) * 512] = np.where((128 * m + r) > c, -400.0, 0.0).astype(ml_dtypes.bfloat16)
    masks[:, 4 * 512:4 * 512 + 128] = np.eye(128, dtype=ml_dtypes.bfloat16)
    masks[:, 4 * 512 + 128:4 * 512 + 256] = (r <= np.arange(128)[None, :]).astype(ml_dtypes.bfloat16)
    ident_f32 = np.eye(128, dtype=np.float32)

    qkv3 = W_qkv.reshape(3, 16, 128, D)
    in_maps = []
    for core in range(NCORES):
        xs = np.ascontiguousarray(x[core * TPC:(core + 1) * TPC, :].T)
        blk = np.empty((2 * 128, DC * 384), dtype=np.float32)
        for hl in range(2):
            h = 2 * core + hl
            for mat in range(3):
                t = qkv3[mat, h].reshape(128, DC, 128).transpose(2, 1, 0)
                blk[hl * 128:(hl + 1) * 128, :].reshape(128, DC, 3, 128)[:, :, mat, :] = t
        gn4 = np.stack([gamma[2 * core], beta[2 * core],
                        gamma[2 * core + 1], beta[2 * core + 1]]).astype(np.float32)
        in_maps.append({
            "x_fm": xs, "wgT": wgT, "wuT": wuT, "wdT": wdT,
            "wqkvT": np.ascontiguousarray(blk), "woT": woT,
            "lam_row": lam_row, "gn4": gn4,
            "consts_bf": masks, "ident_f32": ident_f32,
        })
    return in_maps


def kernel(**inputs):
    nc = build_full(os.environ.get("MK_STOP", "full"))
    in_maps = make_host_inputs(inputs)
    res = run_bass_kernel_spmd(nc, in_maps, core_ids=list(range(NCORES)),
                               trace=os.environ.get("MK_TRACE", "0") == "1")
    out = np.empty((1, S, D), dtype=np.float32)
    for core in range(NCORES):
        out[0, core * TPC:(core + 1) * TPC, :] = res.results[core]["out_fm"].T
    return out


def np_ln(x):
    x = x.astype(np.float64)
    mu = x.mean(-1, keepdims=True)
    v = x.var(-1, keepdims=True)
    return (x - mu) / np.sqrt(v + LN_EPS)


def np_ffn_res(x, wg, wu, wd):
    xn = np_ln(x)
    g = xn @ wg.T.astype(np.float64)
    u = xn @ wu.T.astype(np.float64)
    gated = (g / (1.0 + np.exp(-g))) * u
    return x + 0.5 * (gated @ wd.T.astype(np.float64))



if __name__ == "__main__":
    pass


# version 16 (final)
# speedup vs baseline: 1.0667x; 1.0036x over previous
import math
import os
ATT_SERIAL = os.environ.get("MK_ATT_SERIAL", "0") == "1"
DBG_EPI = os.environ.get("MK_DBG_EPI", "0") == "1"
FLUSH_HOT = False
SHARED_AG = True
from contextlib import ExitStack

import numpy as np
import ml_dtypes

import concourse.bass as bass
import concourse.mybir as mybir
from concourse.bass_utils import run_bass_kernel_spmd
from concourse.aot_env import allow_ici_all_to_all

F32 = mybir.dt.float32
BF16 = mybir.dt.bfloat16
AF = mybir.ActivationFunctionType
ALU = mybir.AluOpType

D = 2048
H = 8192
S = 8192
NCORES = 8
TPC = S // NCORES
DC = D // 128
HCN = H // 128
QB = 512
NQB = S // QB
LN_EPS = 1e-5
LAMBDA_INIT = 0.8 - 0.6 * math.exp(-0.3 * 12)

ENGINES = ("tensor", "vector", "scalar", "gpsimd", "sync")


class Plan:
    def __init__(self, nc, stack):
        self.nc = nc
        self.stack = stack
        self.ops = {e: [] for e in ENGINES}
        self.sems = {}
        self.cnt = {}
        self.rec = {}

    def sem(self, name):
        if name not in self.sems:
            self.sems[name] = self.stack.enter_context(self.nc.semaphore(name))
            self.cnt[name] = 0
        return self.sems[name]

    def val(self, name):
        self.sem(name)
        return self.cnt[name]

    def on(self, eng, f):
        self.ops[eng].append(f)

    def wait(self, eng, name, value):
        if value is None or value <= 0:
            return
        s = self.sem(name)
        self.rec.setdefault(eng, []).append(("wait", name, value))
        self.on(eng, lambda e, s=s, v=value: e.wait_ge(s, v))

    def run(self, eng, f, inc=None, inc_by=1):
        if inc is None:
            self.on(eng, f)
            return None
        s = self.sem(inc)
        self.cnt[inc] += inc_by
        self.rec.setdefault(eng, []).append(("inc", inc, inc_by))
        self.on(eng, lambda e, s=s, f=f, k=inc_by: f(e).then_inc(s, k))
        return self.cnt[inc]

    ENG_SEM = {"tensor": "mm", "vector": "dve", "scalar": "act", "gpsimd": "gp"}

    def runf(self, eng, f):
        c = self.run(eng, f, inc=self.ENG_SEM[eng])
        self.wait(eng, self.ENG_SEM[eng], c)
        return c

    def dma(self, eng, out, in_, inc, **kw):
        s = self.sem(inc)
        self.cnt[inc] += 16
        self.rec.setdefault(eng, []).append(("inc", inc, 16))
        self.on(eng, lambda e, o=out, i=in_, s=s, kw=kw: e.dma_start(out=o, in_=i, **kw).then_inc(s, 16))
        return self.cnt[inc]

    def check_deadlock(self):
        ptr = {e: 0 for e in ENGINES}
        state = {n: 0 for n in self.cnt}
        progress = True
        while progress:
            progress = False
            for e in ENGINES:
                rec = self.rec.get(e, [])
                while ptr[e] < len(rec):
                    kind, name, v = rec[ptr[e]]
                    if kind == "wait" and state.get(name, 0) < v:
                        break
                    if kind == "inc":
                        state[name] = state.get(name, 0) + v
                    ptr[e] += 1
                    progress = True
        stuck = {e: ptr[e] for e in ENGINES if ptr[e] < len(self.rec.get(e, []))}
        if stuck:
            for e, i in stuck.items():
                kind, name, v = self.rec[e][i]
                print(f"DEADLOCK: {e} stuck at event {i}/{len(self.rec[e])}: "
                      f"{kind} {name} >= {v} (state={state.get(name, 0)})")
            raise RuntimeError("deadlock detected in plan")
        print("deadlock check: OK")

    def emit(self):
        nc = self.nc
        with nc.Block() as block:
            @block.tensor
            def _(e):
                for f in self.ops["tensor"]:
                    f(e)

            @block.vector
            def _(e):
                for f in self.ops["vector"]:
                    f(e)

            @block.scalar
            def _(e):
                for f in self.ops["scalar"]:
                    f(e)

            @block.gpsimd
            def _(e):
                for f in self.ops["gpsimd"]:
                    f(e)

            @block.sync
            def _(e):
                for f in self.ops["sync"]:
                    f(e)


class NS:
    pass


def alloc_persistent(p, stack):
    nc = p.nc
    b = NS()
    b.x = stack.enter_context(nc.sbuf_tensor("x_res", [128, DC * TPC], F32))
    b.ones_col = stack.enter_context(nc.sbuf_tensor("ones_col", [128, 1], F32))
    b.ones_col_bf = stack.enter_context(nc.sbuf_tensor("ones_col_bf", [128, 1], BF16))
    b.ones_row = stack.enter_context(nc.sbuf_tensor("ones_row", [1, 128], F32))
    b.rows = stack.enter_context(nc.sbuf_tensor("rows", [1, 5376], F32))
    b.tmp = stack.enter_context(nc.sbuf_tensor("tmp_f32", [128, 1024], F32))
    b.sq = stack.enter_context(nc.sbuf_tensor("sq_f32", [128, 1024], F32))
    return b


def plan_init(p, b):
    p.run("gpsimd", lambda e: e.memset(b.ones_col[:, :], 1.0), inc="init")
    p.run("gpsimd", lambda e: e.memset(b.ones_col_bf[:, :], 1.0), inc="init")
    p.run("gpsimd", lambda e: e.memset(b.ones_row[:, :], 1.0), inc="init")
    p.wait("tensor", "init", 3)
    p.wait("vector", "init", 3)


def plan_load_x(p, b, x_fm):
    src3 = x_fm[:, :].rearrange("(dc p) t -> p dc t", p=128)
    dst3 = b.x[:, :].rearrange("p (dc t) -> p dc t", t=TPC)
    engs = {0: ("sync", "scalar"), 1: ("gpsimd", "sync")}
    cnts = {0: [], 1: []}
    for th in range(2):
        for half in range(2):
            src = src3[:, half * 8:(half + 1) * 8, th * 512:(th + 1) * 512]
            dst = dst3[:, half * 8:(half + 1) * 8, th * 512:(th + 1) * 512]
            sem = "s_x%d%s" % (th, "ab"[half])
            cnts[th].append((sem, p.dma(engs[th][half], dst, src, sem)))
    return cnts


def plan_ln(p, b, a_dst, ps, th_list=(0, 1), xw=None, want_ev=False, bc=None,
            order="dc"):
    p.wait("tensor", "dve", p.val("dve"))
    p.wait("tensor", "act", p.val("act"))
    p.wait("scalar", "dve", p.val("dve"))
    p.wait("vector", "act", p.val("act"))
    last_apply = None
    stats_done = {}

    def emit_stats(th):
        t0 = th * 512
        s1 = ps[4 + th][0:1, :]
        s2 = ps[6 + th][0:1, :]
        if xw is not None and th == 1:
            for sem, c in xw:
                p.wait("scalar", sem, c)
                p.wait("tensor", sem, c)
        p.wait("scalar", "mm", p.val("mm"))
        mm_s2 = [None] * DC
        for dc in range(DC):
            xs = b.x[:, dc * TPC + t0: dc * TPC + t0 + 512]
            sq_sl = b.sq[:, ((th * DC + dc) % 2) * 512:((th * DC + dc) % 2) * 512 + 512]
            if dc >= 2:
                p.wait("scalar", "mm", mm_s2[dc - 2])
            sq_done = p.run("scalar", lambda e, xs=xs, t=sq_sl: e.activation(t, xs, AF.Square), inc="act")
            p.wait("tensor", "act", sq_done)
            start, stop = dc == 0, dc == DC - 1
            p.run("tensor", lambda e, xs=xs, s1=s1, start=start, stop=stop: e.matmul(
                s1, b.ones_col[:, 0:1], xs, start=start, stop=stop), inc=None)
            mm_s2[dc] = p.run("tensor", lambda e, t=sq_sl, s2=s2, start=start, stop=stop: e.matmul(
                s2, b.ones_col[:, 0:1], t, start=start, stop=stop), inc="mm")
        stats_done[th] = mm_s2[DC - 1]

    def emit_rows(th):
        s1 = ps[4 + th][0:1, :]
        s2 = ps[6 + th][0:1, :]
        ro = th * 2560
        R_MEAN = b.rows[0:1, ro + 0:ro + 512]
        R_MSQ = b.rows[0:1, ro + 512:ro + 1024]
        R_VAR = b.rows[0:1, ro + 1024:ro + 1536]
        R_RSTD = b.rows[0:1, ro + 1536:ro + 2048]
        R_B = b.rows[0:1, ro + 2048:ro + 2560]
        p.wait("vector", "mm", stats_done[th])
        p.runf("vector", lambda e, s1=s1, R=R_MEAN: e.tensor_scalar(R, s1, 1.0 / D, None, ALU.mult))
        p.runf("vector", lambda e, s2=s2, R=R_MSQ: e.tensor_scalar(R, s2, 1.0 / D, None, ALU.mult))
        p.runf("vector", lambda e, RM=R_MEAN, RV=R_VAR: e.scalar_tensor_tensor(
            RV, RM, -1.0, RM, ALU.mult, ALU.mult))
        c = p.runf("vector", lambda e, RV=R_VAR, RM=R_MSQ: e.tensor_tensor(RV, RV, RM, ALU.add))
        p.wait("scalar", "dve", c)
        p.runf("scalar", lambda e, RV=R_VAR: e.activation(RV, RV, AF.Ln, bias=np.float32(LN_EPS)))
        c = p.run("scalar", lambda e, RV=R_VAR, RR=R_RSTD: e.activation(
            RR, RV, AF.Exp, scale=np.float32(-0.5)), inc="act")
        p.wait("vector", "act", c)
        c = p.runf("vector", lambda e, RM=R_MEAN, RR=R_RSTD, RB=R_B: e.scalar_tensor_tensor(
            RB, RM, -1.0, RR, ALU.mult, ALU.mult))
        return c, R_RSTD, R_B

    def emit_bc(th, c, R_RSTD, R_B):
        bcA = ps[th * 2][:, :]
        bcB = ps[th * 2 + 1][:, :]
        p.wait("tensor", "dve", c)
        p.run("tensor", lambda e, bcA=bcA, RR=R_RSTD: e.matmul(
            bcA, b.ones_row[0:1, :], RR, start=True, stop=True), inc=None)
        mm_bc = p.run("tensor", lambda e, bcB=bcB, RB=R_B: e.matmul(
            bcB, b.ones_row[0:1, :], RB, start=True, stop=True), inc="mm")
        p.wait("vector", "mm", mm_bc)
        p.run("vector", lambda e, th=th, bcA=bcA: e.tensor_copy(
            bc[:, (th * 2) * 512:(th * 2) * 512 + 512], bcA), inc="dve")
        return p.run("vector", lambda e, th=th, bcB=bcB: e.tensor_copy(
            bc[:, (th * 2 + 1) * 512:(th * 2 + 1) * 512 + 512], bcB), inc="dve")

    def emit_applies():
        nonlocal last_apply
        evs = {}
        if order == "dc":
            pairs = [(dc, thi) for dc in range(DC) for thi in range(len(th_list))]
        else:
            pairs = [(dc, thi) for thi in range(len(th_list)) for dc in range(DC)]
        for idx, (dc, thi) in enumerate(pairs):
            th = th_list[thi]
            if True:
                if xw is not None and th == 1 and dc == 0:
                    for sem, c in xw:
                        p.wait("vector", sem, c)
                t0 = th * 512
                bcA = bc[:, (th * 2) * 512:(th * 2) * 512 + 512]
                bcB = bc[:, (th * 2 + 1) * 512:(th * 2 + 1) * 512 + 512]
                xs = b.x[:, dc * TPC + t0: dc * TPC + t0 + 512]
                asl = a_dst[:, dc * TPC + t0: dc * TPC + t0 + 512]
                sl = idx % 2
                tsl = b.tmp[:, sl * 512:sl * 512 + 512]
                p.runf("vector", lambda e, xs=xs, t=tsl, bcA=bcA: e.tensor_tensor(t, xs, bcA, ALU.mult))
                last_apply = p.run("vector", lambda e, asl=asl, t=tsl, bcB=bcB: e.tensor_tensor(
                    asl, t, bcB, ALU.add), inc="dve")
                evs[(dc, th)] = last_apply
        return evs

    for th in th_list:
        emit_stats(th)
    rowres = {}
    for th in th_list:
        rowres[th] = emit_rows(th)
    for th in th_list:
        emit_bc(th, *rowres[th])
    evs = emit_applies()
    if want_ev:
        return last_apply, evs
    return last_apply


def plan_ffn(p, b, sb, a_bf, dram, first_call, a_ready_dve, between_th=None,
             out_stream=None, a_ev=None):
    nc = p.nc
    wgu = sb["wgu"]
    NB = TPC // 512
    pname = "1" if first_call else "2"

    def wsl(wpar, mat, dc):
        base = ((wpar * 2) + mat) * D
        return wgu[:, base + dc * 128: base + dc * 128 + 128]

    if a_ev is None:
        p.wait("tensor", "dve", a_ready_dve)
    if not first_call:
        p.wait("gpsimd", "mmo", p.val("mmo"))
    with ExitStack() as _ps:
        ps_g = [[_ps.enter_context(nc.psum_tensor("psg%d%d" % (par, nb) + pname, [128, 512], F32))
                 for nb in range(2)] for par in range(2)]
        ps_u = [[_ps.enter_context(nc.psum_tensor("psu%d%d" % (par, nb) + pname, [128, 512], F32))
                 for nb in range(2)] for par in range(2)]
        mm_done = [None] * HCN
        ev_done = [None] * HCN
        st_done = [None] * HCN
        if a_ev is None:
            p.wait("tensor", "act", p.val("act"))
        for h in range(HCN):
            par = h % 2
            wpar = h % 4
            if h >= 4:
                p.wait("gpsimd", "mmg", mm_done[h - 4])
            w_cnt = None
            for mat, wname in enumerate(("wgT", "wuT")):
                wt = dram[wname]
                src = wt[h * 128:(h + 1) * 128, :]
                dst = wgu[:, ((wpar * 2) + mat) * D: ((wpar * 2) + mat) * D + D]
                w_cnt = p.dma("gpsimd", dst, src, "w%d" % wpar + pname)
            if first_call:
                hc = h
                pp = hc % 2
                prep = sb["prep"]
                if hc >= 2:
                    p.wait("gpsimd", "prep_st%d" % pp, 64 * (hc // 2))
                ld = p.dma("gpsimd", prep[:, pp * D:(pp + 1) * D],
                           dram["wdT"][hc * 128:(hc + 1) * 128, :], "prep_ld%d" % pp)
                p.wait("sync", "prep_ld%d" % pp, ld)
                g8i, sl = hc // 8, hc % 8
                for qq in range(4):
                    p.dma("sync", dram["wd_bf"][(qq * 8 + g8i) * 128:(qq * 8 + g8i + 1) * 128,
                                                sl * 512:(sl + 1) * 512],
                          prep[:, pp * D + qq * 512: pp * D + (qq + 1) * 512], "prep_st%d" % pp)
            p.wait("tensor", "w%d" % wpar + pname, w_cnt)
            if h >= 2:
                p.wait("tensor", "dve", ev_done[h - 2])
            for mat in range(2):
                pst = (ps_g if mat == 0 else ps_u)[par]
                for dc in range(DC):
                    for nb in range(NB):
                        if a_ev is not None and h == 0 and mat == 0:
                            p.wait("tensor", "dve", a_ev[(dc, nb)])
                        start, stop = dc == 0, dc == DC - 1
                        last = mat == 1 and nb == NB - 1 and dc == DC - 1
                        f = lambda e, pst=pst, nb=nb, wpar=wpar, mat=mat, dc=dc, start=start, stop=stop: e.matmul(
                            pst[nb][:, :], wsl(wpar, mat, dc),
                            a_bf[:, dc * TPC + nb * 512: dc * TPC + (nb + 1) * 512],
                            start=start, stop=stop)
                        if last:
                            mm_done[h] = p.run("tensor", f, inc="mmg")
                        else:
                            p.run("tensor", f, inc=None)
            p.wait("scalar", "mmg", mm_done[h])
            if h >= 2:
                p.wait("scalar", "dve", ev_done[h - 2])
            for nb in range(NB):
                c = p.run("scalar", lambda e, par=par, nb=nb: e.activation(
                    sb["silu"][:, par * TPC + nb * 512: par * TPC + (nb + 1) * 512],
                    ps_g[par][nb][:, :], AF.Silu), inc="act")
            p.wait("vector", "act", c)
            if h >= 2:
                p.wait("vector", "g_st%d" % par + pname, st_done[h - 2])
            for nb in range(NB):
                ev_done[h] = p.run("vector", lambda e, par=par, nb=nb: e.scalar_tensor_tensor(
                    sb["gated"][:, par * TPC + nb * 512: par * TPC + (nb + 1) * 512],
                    ps_u[par][nb][:, :], 0.5,
                    sb["silu"][:, par * TPC + nb * 512: par * TPC + (nb + 1) * 512],
                    ALU.mult, ALU.mult), inc="dve")
            p.wait("sync", "dve", ev_done[h])
            g4i, sl4 = h // 4, h % 4
            st_done[h] = p.dma("sync", dram["gated"][g4i * 128:(g4i + 1) * 128,
                                                     sl4 * TPC:(sl4 + 1) * TPC],
                               sb["gated"][:, par * TPC:(par + 1) * TPC], "g_st%d" % par + pname)
        gu_last_ev = ev_done[HCN - 1]
    gst_tot = (p.val("g_st0" + pname), p.val("g_st1" + pname))

    wd_ready = (p.val("prep_st0"), p.val("prep_st1"))
    p.wait("tensor", "dve", gu_last_ev)
    gst_tot = (p.val("g_st0" + pname), p.val("g_st1" + pname))
    mmd_base = p.val("mmd")
    bank_guard = {}
    split = between_th is not None
    groups = [((0,), q) for q in range(4)] + [((1,), q) for q in range(4)] if split \
        else [((0, 1), q) for q in range(4)]
    with ExitStack() as _ps:
        ps_d = [_ps.enter_context(nc.psum_tensor("psd%d" % i + pname, [128, 512], F32))
                for i in range(8)]
        wd_in = sb["wd_in"]
        g_in = sb["g_in"]
        mmd_per_hc = 4 * (1 if split else 2) // 4
        for gi, (ths, q) in enumerate(groups):
            nth = len(ths)
            if split:
                bs = (gi % 2) * 4 if gi < 4 else 0
            else:
                bs = 0
            wldc = [None] * 8
            mm_hc = [None] * HCN
            for g8 in range(8):
                li = gi * 8 + g8
                if gi == 0 and g8 == 0 and first_call:
                    p.wait("scalar", "prep_st0", wd_ready[0])
                    p.wait("scalar", "prep_st1", wd_ready[1])
                prev = li - 2
                if prev >= 0:
                    p.wait("scalar", "mmd", mmd_base + (prev // 8) * HCN + (prev % 8) * 8 + 8)
                srcw = dram["wd_bf"][(q * 8 + g8) * 128:(q * 8 + g8 + 1) * 128, :]
                dstw = wd_in[:, (li % 2) * 4096:(li % 2) * 4096 + 4096]
                wldc[g8] = p.dma("scalar", dstw, srcw, "wd%d" % (li % 2) + pname)
            gldc = [None] * 16
            for g4 in range(16):
                lj = gi * 16 + g4
                if gi == 0 and g4 == 0:
                    p.wait("sync", "g_st0" + pname, gst_tot[0])
                    p.wait("sync", "g_st1" + pname, gst_tot[1])
                prev = lj - 2
                if prev >= 0:
                    p.wait("sync", "mmd", mmd_base + (prev // 16) * HCN + (prev % 16) * 4 + 4)
                if split:
                    th = ths[0]
                    srcg = dram["gated"][g4 * 128:(g4 + 1) * 128, :].rearrange(
                        "p (s t) -> p s t", t=TPC)[:, :, th * 512:(th + 1) * 512]
                    dstg = g_in[:, (lj % 2) * 2048:(lj % 2) * 2048 + 2048].rearrange(
                        "p (s t) -> p s t", t=512)
                else:
                    srcg = dram["gated"][g4 * 128:(g4 + 1) * 128, :]
                    dstg = g_in[:, (lj % 2) * 4096:(lj % 2) * 4096 + 4096]
                gldc[g4] = p.dma("sync", dstg, srcg, "gl%d" % (lj % 2) + pname)
            for hc in range(HCN):
                li = gi * 8 + hc // 8
                lj = gi * 16 + hc // 4
                if hc % 8 == 0:
                    p.wait("tensor", "wd%d" % (li % 2) + pname, wldc[hc // 8])
                if hc % 4 == 0:
                    p.wait("tensor", "gl%d" % (lj % 2) + pname, gldc[hc // 4])
                if hc == 0:
                    p.wait("tensor", "dve", bank_guard.get(bs, 0))
                gw = (lj % 2) * (2048 if split else 4096)
                for dt in range(4):
                    for thi, th in enumerate(ths):
                        start, stop = hc == 0, hc == HCN - 1
                        lastmm = dt == 3 and thi == nth - 1
                        if split:
                            rh = g_in[:, gw + (hc % 4) * 512: gw + (hc % 4) * 512 + 512]
                        else:
                            rh = g_in[:, gw + (hc % 4) * 1024 + th * 512:
                                      gw + (hc % 4) * 1024 + th * 512 + 512]
                        f = lambda e, bs=bs, li=li, hc=hc, dt=dt, thi=thi, nth=nth, rh=rh, start=start, stop=stop: e.matmul(
                            ps_d[bs + dt * nth + thi][:, :],
                            wd_in[:, (li % 2) * 4096 + (hc % 8) * 512 + dt * 128:
                                  (li % 2) * 4096 + (hc % 8) * 512 + dt * 128 + 128],
                            rh, start=start, stop=stop)
                        if lastmm:
                            mm_hc[hc] = p.run("tensor", f, inc="mmd")
                        else:
                            p.run("tensor", f, inc=None)
            p.wait("vector", "mmd", mm_hc[HCN - 1])
            for dt in range(4):
                for thi, th in enumerate(ths):
                    xsl = b.x[:, (q * 4 + dt) * TPC + th * 512:(q * 4 + dt) * TPC + (th + 1) * 512]
                    p.run("vector", lambda e, xsl=xsl, bs=bs, dt=dt, thi=thi, nth=nth: e.tensor_tensor(
                        xsl, ps_d[bs + dt * nth + thi][:, :], xsl, ALU.add), inc="dve")
            bank_guard[bs] = p.val("dve")
            if out_stream is not None:
                ev = p.val("dve")
                osrc = b.x[:, q * 4 * TPC:(q + 1) * 4 * TPC].rearrange(
                    "p (dc t) -> p dc t", t=TPC)
                odst = out_stream[:, :].rearrange(
                    "(dc p) t -> p dc t", p=128)[:, q * 4:(q + 1) * 4, :]
                if gi < len(groups) - 1:
                    p.wait("gpsimd", "dve", ev)
                    p.dma("gpsimd", odst, osrc, "s_out")
                else:
                    for hf, eng in enumerate(("gpsimd", "sync")):
                        p.wait(eng, "dve", ev)
                        p.dma(eng, odst[:, hf * 2:(hf + 1) * 2, :],
                              osrc[:, hf * 2:(hf + 1) * 2, :],
                              "s_out" if eng == "gpsimd" else "s_out_s")
            if split and gi == 3 and between_th is not None:
                between_th(ps_d)()
    return p.val("dve")


def plan_store_a(p, a_bf, dst_dram, th, eng, ev):
    p.wait(eng, "dve", ev)
    src = a_bf[:, :].rearrange("p (dc t) -> p dc t", t=TPC)[:, :, th * 512:(th + 1) * 512]
    dst = dst_dram[:, :].rearrange("p (dc c) -> p dc c", c=512)
    return p.dma(eng, dst, src, "a_st%d" % th)


def plan_collective(p, kind, ins, outs, sem_name, wait_dma=None):
    if wait_dma is not None:
        p.wait("gpsimd", wait_dma[0], wait_dma[1])
    s = p.sem(sem_name)
    p.cnt[sem_name] += 1
    p.rec.setdefault("gpsimd", []).append(("inc", sem_name, 1))

    def f(e, kind=kind, ins=ins, outs=outs, s=s):
        if kind == "AllToAll":
            with allow_ici_all_to_all():
                return e.collective_compute(kind, ALU.bypass, replica_groups=[list(range(NCORES))],
                                            ins=ins, outs=outs).then_inc(s)
        return e.collective_compute(kind, ALU.bypass, replica_groups=[list(range(NCORES))],
                                    ins=ins, outs=outs).then_inc(s)
    p.on("gpsimd", f)
    return p.cnt[sem_name]


def plan_attn_prep(p, b, at, dram):
    LB = 4608
    Lr = lambda a, z: b.rows[0:1, LB + a:LB + z]
    c_lam = p.dma("sync", Lr(0, 512), dram["lam_row"][:, :], "prep2")
    c_gn = p.dma("sync", at["gbpad"][0:4, :], dram["gn4"][:, :], "prep2")
    c_msk = p.dma("sync", at["masks"][:, :], dram["consts_bf"][:, :], "prep2")
    c_idf = p.dma("sync", at["identf"][:, :], dram["ident_f32"][:, :], "prep2")
    p.wait("vector", "prep2", c_idf)
    c = p.runf("vector", lambda e: e.tensor_tensor(Lr(512, 640), Lr(0, 128), Lr(128, 256), ALU.mult))
    c = p.runf("vector", lambda e: e.tensor_tensor(Lr(640, 768), Lr(256, 384), Lr(384, 512), ALU.mult))
    p.wait("scalar", "dve", c)
    c = p.run("scalar", lambda e: e.activation(Lr(0, 256), Lr(512, 768), AF.Exp), inc="act")
    p.wait("vector", "act", c)
    c = p.runf("vector", lambda e: e.tensor_tensor(Lr(256, 384), Lr(0, 128), Lr(128, 256), ALU.subtract))
    lam_done = p.runf("vector", lambda e: e.tensor_scalar(
        Lr(256, 384), Lr(256, 384), LAMBDA_INIT, None, ALU.add))
    at["lam_row128"] = Lr(256, 384)
    p.wait("tensor", "prep2", c_idf)
    c = p.run("tensor", lambda e: e.transpose(at["ps_sp"][:, 0:128], at["gbpad"][:, :], at["identf"][:, :]), inc="mA")
    p.wait("tensor", "dve", lam_done)
    c2 = p.run("tensor", lambda e: e.transpose(at["ps_sp"][:, 128:129], at["lam_row128"],
                                               at["identf"][0:1, 0:1]), inc="mA")
    p.wait("vector", "mA", c2)
    p.runf("vector", lambda e: e.tensor_scalar(
        at["gb"][:, :], at["ps_sp"][:, 0:4], 1.0 - LAMBDA_INIT, None, ALU.mult))
    p.runf("vector", lambda e: e.tensor_copy(at["lamc"][:, :], at["ps_sp"][:, 128:129]))
    return c_idf


def plan_qkv_head(p, b, at, hl, dram, psq, ag1_done):
    p.wait("gpsimd", "mq", p.val("mq"))
    src = dram["wqkvT"][hl * 128:(hl + 1) * 128, :]
    dst = at["wq"][:, :]
    wq_cnt = p.dma("gpsimd", dst, src, "wqkv")
    p.wait("sync", "a_st0", dram.get("_a_st_a"))
    p.wait("sync", "a_st1", dram.get("_a_st_b"))
    p.wait("sync", "mq", p.val("mq"))
    p.wait("tensor", "prep2", p.val("prep2"))
    p.wait("vector", "h_st0", p.val("h_st0"))
    p.wait("vector", "h_st1", p.val("h_st1"))
    ev_v = [None] * 16
    tp_done = [None] * 16
    mq_done = [None] * 16
    order = [2 * i for i in range(8)] + [2 * i + 1 for i in range(8)]
    for oi, tc in enumerate(order):
        pi = oi % 2
        r, half = tc // 2, tc % 2
        if oi >= 2:
            p.wait("sync", "mq", mq_done[order[oi - 2]])
        if oi == 0:
            p.wait("sync", "cc1a", ag1_done[0])
        if oi == 8:
            p.wait("sync", "cc1b", ag1_done[1])
        srcn = dram["ag1_out_" + ("a" if half == 0 else "b")]
        src = srcn[r * 128:(r + 1) * 128, :].rearrange("p (dc c) -> p dc c", c=512)
        dst = at["chunk"][:, pi * 8192:(pi + 1) * 8192].rearrange("p (dc t) -> p dc t", t=512)
        ld = p.dma("sync", dst, src, "aq%d" % pi)
        p.wait("tensor", "aq%d" % pi, ld)
        if oi == 0:
            p.wait("tensor", "wqkv", wq_cnt)
        if oi >= 2:
            p.wait("tensor", "dve", ev_v[order[oi - 2]])
            p.wait("tensor", "act", p.val("act"))
        mq = None
        for mat in range(3):
            T = psq["T"][mat][pi]
            for dc in range(DC):
                start, stop = dc == 0, dc == DC - 1
                f = lambda e, T=T, pi=pi, mat=mat, dc=dc, start=start, stop=stop: e.matmul(
                    T[:, :],
                    at["wq"][:, dc * 384 + mat * 128: dc * 384 + mat * 128 + 128],
                    at["chunk"][:, pi * 8192 + dc * 512: pi * 8192 + (dc + 1) * 512],
                    start=start, stop=stop)
                last = mat == 2 and dc == DC - 1
                if last:
                    mq = p.run("tensor", f, inc="mq")
                else:
                    p.run("tensor", f, inc=None)
        mq_done[tc] = mq
        p.wait("vector", "mq", mq)
        p.run("vector", lambda e, tc=tc, pi=pi: e.tensor_copy(
            at["q_fm"][:, tc * 512:(tc + 1) * 512], psq["T"][0][pi][:, :]), inc=None)
        if oi >= 2:
            p.wait("vector", "mq", tp_done[order[oi - 2]][0])
        ev_v[tc] = p.run("vector", lambda e, tc=tc, pi=pi: e.tensor_copy(
            at["v_st"][:, pi * 512:(pi + 1) * 512], psq["T"][2][pi][:, :]), inc="dve")
        p.wait("scalar", "mq", mq)
        p.run("scalar", lambda e, tc=tc, pi=pi: e.activation(
            at["k_fm"][:, tc * 512:(tc + 1) * 512], psq["T"][1][pi][:, :], AF.Copy,
            bias=0.0), inc="act")
        p.wait("tensor", "dve", ev_v[tc])
        if oi >= 2:
            p.wait("tensor", "dve", tp_done[order[oi - 2]][1])
        tmm = None
        for i in range(4):
            tmm = p.run("tensor", lambda e, pi=pi, i=i: e.transpose(
                psq["ps_t"][pi][:, i * 128:(i + 1) * 128],
                at["v_st"][:, pi * 512 + i * 128: pi * 512 + (i + 1) * 128],
                at["ident_bf"][:, :]), inc="mq")
        p.wait("vector", "mq", tmm)
        tcopy = p.run("vector", lambda e, tc=tc, pi=pi: e.tensor_copy(
            at["v_tm"][:, tc * 512:(tc + 1) * 512], psq["ps_t"][pi][:, 0:512]), inc="dve")
        tp_done[tc] = (tmm, tcopy)
    return tp_done[order[-1]][1]


def plan_attn_head(p, b, at, hl, dram, psa, qkv_done, dbg=None):
    sA, sB, aC, dD = psa
    scale = np.float32(0.125)
    p.wait("tensor", "dve", qkv_done)
    p.wait("tensor", "act", p.val("act"))
    p.wait("scalar", "mA", p.val("mA"))
    p.wait("scalar", "dve", p.val("dve"))
    p.wait("vector", "mA", p.val("mA"))
    e2 = {}
    m2 = {}
    s_cnt = {}
    av_cnt = {}
    acc_cnt = {}
    gof = [0]
    st_prev = [None, None]
    ev_state = {"a1": None, "a2": None}
    NEB = 6

    def emit_scores(j, kv, diag_m):
        g = gof[0] + kv
        sp = kv % 2
        qsl = slice(j * QB, (j + 1) * QB)
        if g >= 2 and (g - 2) in e2:
            p.wait("tensor", "act", e2[g - 2])
        for br in range(2):
            ps = (sA if br == 0 else sB)[sp][:, :]
            lh = at["k_fm"][br * 64:(br + 1) * 64, kv * 128:(kv + 1) * 128]
            rh = at["q_fm"][br * 64:(br + 1) * 64, qsl]
            f = lambda e, ps=ps, lh=lh, rh=rh, stop=diag_m is None: e.matmul(
                ps, lh, rh, start=True, stop=stop)
            if diag_m is None and br == 1:
                s_cnt[g] = p.run("tensor", f, inc="mA")
            else:
                p.run("tensor", f, inc=None)
            if diag_m is not None:
                fm = lambda e, ps=ps, m=diag_m: e.matmul(
                    ps, at["tri_bf"][:, :],
                    at["masks"][:, m * 512:(m + 1) * 512], start=False, stop=True)
                if br == 1:
                    s_cnt[g] = p.run("tensor", fm, inc="mA")
                else:
                    p.run("tensor", fm, inc=None)

    def emit_exp(j, kv, diag_m):
        g = gof[0] + kv
        sp = kv % 2
        bi = g % NEB
        p.wait("scalar", "mA", s_cnt[g])
        if g >= NEB and (g - NEB) in av_cnt:
            p.wait("scalar", "mA", av_cnt[g - NEB])
            p.wait("scalar", "dve", acc_cnt[g - NEB])
        p.run("scalar", lambda e, sp=sp, bi=bi: e.activation(
            at["exp"][:, bi * 512:bi * 512 + 512],
            sA[sp][:, :], AF.Exp, scale=scale), inc="act")
        e2[g] = p.run("scalar", lambda e, sp=sp, bi=bi: e.activation(
            at["exp"][:, (NEB + bi) * 512:(NEB + bi) * 512 + 512],
            sB[sp][:, :], AF.Exp, scale=scale), inc="act")
        p.wait("vector", "act", e2[g])
        rv = (lambda eng, f: p.runf(eng, f)) if FLUSH_HOT else (lambda eng, f: p.run(eng, f, inc="dve"))
        for br in range(2):
            esl = at["exp"][:, (br * NEB + bi) * 512:(br * NEB + bi) * 512 + 512]
            accsl = at["acc"][:, br * 512:(br + 1) * 512]
            if kv == 0:
                p.wait("vector", "dve", ev_state.get("c2"))
                c = rv("vector", lambda e, accsl=accsl, esl=esl: e.tensor_copy(accsl, esl))
            else:
                c = rv("vector", lambda e, accsl=accsl, esl=esl: e.tensor_tensor(
                    accsl, accsl, esl, ALU.add))
            if br == 1:
                acc_cnt[g] = c

    def emit_av(j, kv, diag, last):
        g = gof[0] + kv
        bi = g % NEB
        p.wait("tensor", "act", e2[g])
        if kv == 0:
            p.wait("tensor", "dve", ev_state["a2"])
        for br in range(2):
            ps = aC[br][:, :]
            lh = at["v_tm"][:, kv * 128:(kv + 1) * 128]
            rh = at["exp"][:, (br * NEB + bi) * 512:(br * NEB + bi) * 512 + 512]
            f = lambda e, ps=ps, lh=lh, rh=rh, start=kv == 0, stop=last: e.matmul(
                ps, lh, rh, start=start, stop=stop)
            if br == 1:
                av_cnt[g] = p.run("tensor", f, inc="mA")
            else:
                p.run("tensor", f, inc=None)

    def make_epilogue(j):
        A1s = b.tmp[:, 0:512]
        A2s = b.tmp[:, 512:1024]
        rho_b = b.sq[:, 0:512]
        osq = b.sq[:, 512:1024]
        R1 = b.rows[0:1, 0:512]
        R2 = b.rows[0:1, 512:1024]
        R3 = b.rows[0:1, 1024:1536]
        R4 = b.rows[0:1, 1536:2048]
        R5 = b.rows[0:1, 2048:2560]
        R6 = b.rows[0:1, 2560:3072]
        R7 = b.rows[0:1, 3072:3584]
        st = {}

        av_last = av_cnt[gof[0] - 1]
        p.wait("vector", "mA", av_last)
        ev_state["a1"] = None
        c = p.runf("vector", lambda e: e.tensor_copy(A1s, aC[0][:, :]))
        ev_state["a1"] = c
        ev_state["a2"] = p.runf("vector", lambda e: e.tensor_copy(A2s, aC[1][:, :]))
        p.run("vector", lambda e: e.tensor_copy(at["accbf"][:, 0:512], at["acc"][:, 0:512]), inc=None)
        st["c2"] = p.run("vector", lambda e: e.tensor_copy(at["accbf"][:, 512:1024], at["acc"][:, 512:1024]), inc="dve")
        ev_state["c2"] = st["c2"]

        def round1():
            p.wait("tensor", "dve", st["c2"])
            p.run("tensor", lambda e: e.matmul(dD[0][0:1, :], b.ones_col_bf[:, 0:1],
                                               at["accbf"][:, 0:512], start=True, stop=True), inc=None)
            cd = p.run("tensor", lambda e: e.matmul(dD[1][0:1, :], b.ones_col_bf[:, 0:1],
                                                    at["accbf"][:, 512:1024], start=True, stop=True), inc="mA")
            p.wait("scalar", "mA", cd)
            p.runf("scalar", lambda e: e.activation(R1, dD[0][0:1, :], AF.Ln))
            c = p.run("scalar", lambda e: e.activation(R2, dD[1][0:1, :], AF.Ln), inc="act")
            p.wait("vector", "act", c)
            c = p.runf("vector", lambda e: e.tensor_tensor(R2, R1, R2, ALU.subtract))
            p.wait("scalar", "dve", c)
            st["rho"] = p.run("scalar", lambda e: e.activation(R2, R2, AF.Exp), inc="act")

        def round2():
            p.wait("tensor", "act", st["rho"])
            crb = p.run("tensor", lambda e: e.matmul(dD[0][:, :], b.ones_row[0:1, :], R2,
                                                     start=True, stop=True), inc="mA")
            p.wait("vector", "mA", crb)
            p.runf("vector", lambda e: e.tensor_copy(rho_b, dD[0][:, :]))
            p.runf("vector", lambda e: e.tensor_tensor(A2s, A2s, rho_b, ALU.mult))
            p.runf("vector", lambda e: e.tensor_scalar(A2s, A2s, at["lamc"][:, 0:1], None, ALU.mult))
            p.runf("vector", lambda e: e.tensor_tensor(A1s, A1s, A2s, ALU.subtract))
            st["o"] = p.runf("vector", lambda e: e.tensor_tensor(osq, A1s, A1s, ALU.mult))

        def round3():
            p.wait("tensor", "dve", st["o"])
            p.run("tensor", lambda e: e.matmul(dD[0][0:1, :], b.ones_col[:, 0:1], A1s,
                                               start=True, stop=True), inc=None)
            cs = p.run("tensor", lambda e: e.matmul(dD[1][0:1, :], b.ones_col[:, 0:1], osq,
                                                    start=True, stop=True), inc="mA")
            p.wait("vector", "mA", cs)
            p.runf("vector", lambda e: e.tensor_scalar(R3, dD[0][0:1, :], 1.0 / 128.0, None, ALU.mult))
            p.runf("vector", lambda e: e.tensor_scalar(R4, dD[1][0:1, :], 1.0 / 128.0, None, ALU.mult))
            p.runf("vector", lambda e: e.scalar_tensor_tensor(R5, R3, -1.0, R3, ALU.mult, ALU.mult))
            cv = p.runf("vector", lambda e: e.tensor_tensor(R5, R5, R4, ALU.add))
            p.wait("scalar", "dve", cv)
            p.runf("scalar", lambda e: e.activation(R6, R5, AF.Ln, bias=np.float32(LN_EPS)))
            cq = p.run("scalar", lambda e: e.activation(R6, R6, AF.Exp, scale=np.float32(-0.5)), inc="act")
            p.wait("vector", "act", cq)
            st["gb"] = p.runf("vector", lambda e: e.scalar_tensor_tensor(R7, R3, -1.0, R6, ALU.mult, ALU.mult))

        def round4():
            p.wait("tensor", "dve", st["gb"])
            p.run("tensor", lambda e: e.matmul(dD[0][:, :], b.ones_row[0:1, :], R6,
                                               start=True, stop=True), inc=None)
            cab = p.run("tensor", lambda e: e.matmul(dD[1][:, :], b.ones_row[0:1, :], R7,
                                                     start=True, stop=True), inc="mA")
            p.wait("vector", "mA", cab)
            p.runf("vector", lambda e: e.tensor_tensor(at["on"][:, 0:512], A1s, dD[0][:, :], ALU.mult))
            p.runf("vector", lambda e: e.tensor_tensor(at["on"][:, 0:512], at["on"][:, 0:512], dD[1][:, :], ALU.add))
            pj = j % 2
            if st_prev[pj] is not None:
                p.wait("vector", "h_st%d" % pj, st_prev[pj])
            cfin = p.run("vector", lambda e, pj=pj: e.tensor_scalar(
                at["head_bf"][:, pj * 512:(pj + 1) * 512], at["on"][:, 0:512],
                at["gb"][:, hl * 2:hl * 2 + 1], at["gb"][:, hl * 2 + 1:hl * 2 + 2],
                ALU.mult, ALU.add), inc="dve")
            p.wait("sync", "dve", cfin)
            st_prev[pj] = p.dma("sync", dram["a2a_in"][
                                (j // 2) * 256 + hl * 128:(j // 2) * 256 + hl * 128 + 128,
                                (j % 2) * 512:(j % 2) * 512 + 512],
                                at["head_bf"][:, pj * 512:(pj + 1) * 512], "h_st%d" % pj)
        return [round1, round2, round3, round4]

    pend = []
    for j in range(NQB):
        ktiles = 4 * (j + 1)
        for kv in range(ktiles):
            diag_m = kv - 4 * j if kv >= 4 * j else None
            emit_scores(j, kv, diag_m)
            emit_exp(j, kv, diag_m)
            if kv >= 3:
                emit_av(j, kv - 3, (kv - 3) >= 4 * j, False)
            if pend and kv >= 1:
                pend.pop(0)()
        for t in (3, 2):
            emit_av(j, ktiles - t, (ktiles - t) >= 4 * j, False)
        emit_av(j, ktiles - 1, True, True)
        while pend:
            pend.pop(0)()
        gof[0] += ktiles
        pend = make_epilogue(j)
    while pend:
        pend.pop(0)()
    return (p.val("h_st0"), p.val("h_st1"))



def plan_wo(p, b, dram, a2a_done):
    nc = p.nc
    with nc.sbuf_tensor("wo_rhs", [128, DC * TPC], BF16) as wo_rhs, \
         nc.sbuf_tensor("wo_w", [128, 2 * D], BF16) as wo_w, \
         nc.psum_tensor("pso0", [128, 1024], F32) as o0, \
         nc.psum_tensor("pso1", [128, 1024], F32) as o1:
        pso = [o0, o1]
        src3 = dram["a2a_out"][:, :].rearrange("(hc p) t -> p hc t", p=128)
        dst3 = wo_rhs[:, :].rearrange("p (hc t) -> p hc t", t=TPC)
        rhs_cnt = {}
        for half, eng in enumerate(("sync", "scalar")):
            p.wait(eng, "cc2", 1)
            rhs_cnt[half] = p.dma(eng, dst3[:, half * 8:(half + 1) * 8, :],
                                  src3[:, half * 8:(half + 1) * 8, :],
                                  "wo_rhs_ld%d" % half)
        mmo = [None] * DC
        evo = [None] * DC
        for do in range(DC):
            par = do % 2
            if do >= 2:
                p.wait("gpsimd", "mmo", mmo[do - 2])
            src = dram["woT"][do * 128:(do + 1) * 128, :]
            dst = wo_w[:, par * D:(par + 1) * D]
            wcnt = p.dma("gpsimd", dst, src, "wo_w_ld%d" % par)
            p.wait("tensor", "wo_w_ld%d" % par, wcnt)
            if do == 0:
                p.wait("tensor", "wo_rhs_ld0", rhs_cnt[0])
                p.wait("tensor", "wo_rhs_ld1", rhs_cnt[1])
                p.wait("tensor", "dve", p.val("dve"))
                p.wait("tensor", "act", p.val("act"))
            if do >= 2:
                p.wait("tensor", "dve", evo[do - 2])
            for hci, hc in enumerate(range(DC)):
                for nb in range(2):
                    start, stop = hci == 0, hci == DC - 1
                    last = nb == 1 and hci == DC - 1
                    f = lambda e, par=par, nb=nb, hc=hc, do=do, start=start, stop=stop: e.matmul(
                        pso[par][:, nb * 512:(nb + 1) * 512],
                        wo_w[:, par * D + hc * 128: par * D + hc * 128 + 128],
                        wo_rhs[:, hc * TPC + nb * 512: hc * TPC + (nb + 1) * 512],
                        start=start, stop=stop)
                    if last:
                        mmo[do] = p.run("tensor", f, inc="mmo")
                    else:
                        p.run("tensor", f, inc=None)
            p.wait("vector", "mmo", mmo[do])
            xsl = b.x[:, do * TPC:(do + 1) * TPC]
            evo[do] = p.run("vector", lambda e, xsl=xsl, par=par: e.tensor_tensor(
                xsl, pso[par][:, :], xsl, ALU.add), inc="dve")
    return p.val("dve")


def build_full(stop_after="full", skip_ffn1=False):
    nc = bass.Bass("TRN2", target_bir_lowering=False, debug=False)
    dram = {}
    x_fm = nc.dram_tensor("x_fm", [D, TPC], F32, kind="ExternalInput").ap()
    dram["wgT"] = nc.dram_tensor("wgT", [HCN * 128, D], F32, kind="ExternalInput").ap()
    dram["wuT"] = nc.dram_tensor("wuT", [HCN * 128, D], F32, kind="ExternalInput").ap()
    dram["wdT"] = nc.dram_tensor("wdT", [H, D], F32, kind="ExternalInput").ap()
    dram["wqkvT"] = nc.dram_tensor("wqkvT", [2 * 128, DC * 384], F32, kind="ExternalInput").ap()
    dram["woT"] = nc.dram_tensor("woT", [DC * 128, D], F32, kind="ExternalInput").ap()
    dram["lam_row"] = nc.dram_tensor("lam_row", [1, 512], F32, kind="ExternalInput").ap()
    dram["gn4"] = nc.dram_tensor("gn4", [4, 128], F32, kind="ExternalInput").ap()
    dram["consts_bf"] = nc.dram_tensor("consts_bf", [128, 4 * 512 + 256], BF16, kind="ExternalInput").ap()
    dram["ident_f32"] = nc.dram_tensor("ident_f32", [128, 128], F32, kind="ExternalInput").ap()
    out_fm = nc.dram_tensor("out_fm", [D, TPC], F32, kind="ExternalOutput").ap()
    dbg = nc.dram_tensor("dbg", [3 * 128, S], BF16, kind="ExternalOutput").ap()         if stop_after in ("qkv", "attn", "ag1") else None

    dram["gated"] = nc.dram_tensor("gated_scr", [16 * 128, 4 * TPC], BF16).ap()
    dram["wd_bf"] = nc.dram_tensor("wd_bf_scr", [32 * 128, 8 * 512], BF16).ap()
    kw = {"addr_space": "Shared"} if SHARED_AG else {}
    for hf in "ab":
        dram["ag1_in_" + hf] = nc.dram_tensor("ag1_in_" + hf, [128, DC * 512], BF16).ap()
        dram["ag1_out_" + hf] = nc.dram_tensor("ag1_out_" + hf, [NCORES * 128, DC * 512], BF16, **kw).ap()
    dram["a2a_in"] = nc.dram_tensor("a2a_in", [2 * NCORES * 128, TPC], BF16).ap()
    dram["a2a_out"] = nc.dram_tensor("a2a_out", [2 * NCORES * 128, TPC], BF16).ap()

    with ExitStack() as stack:
        p = Plan(nc, stack)
        b = alloc_persistent(p, stack)
        plan_init(p, b)
        xc = plan_load_x(p, b, x_fm)
        for e in ("scalar", "tensor", "vector"):
            for sem, c in xc[0]:
                p.wait(e, sem, c)

        with ExitStack() as f1:
            a_bf = f1.enter_context(nc.sbuf_tensor("a1_bf", [128, DC * TPC], BF16))
            sb = {
                "wgu": f1.enter_context(nc.sbuf_tensor("wgu1", [128, 4 * 2 * D], BF16)),
                "silu": f1.enter_context(nc.sbuf_tensor("silu1", [128, 2 * TPC], BF16)),
                "gated": f1.enter_context(nc.sbuf_tensor("gated1", [128, 2 * TPC], BF16)),
                "prep": f1.enter_context(nc.sbuf_tensor("prep1", [128, 2 * D], BF16)),
                "wd_in": f1.enter_context(nc.sbuf_tensor("wd_in1", [128, 2 * 8 * 512], BF16)),
                "g_in": f1.enter_context(nc.sbuf_tensor("g_in1", [128, 2 * 4 * 1024], BF16)),
            }
            with ExitStack() as _ps:
                lps = [_ps.enter_context(nc.psum_tensor("lnp%d" % i, [128, 512], F32)) for i in range(8)]
                bc1 = sb["g_in"][:, 0:4096].bitcast(F32)
                a_ready, a_ev1 = plan_ln(p, b, a_bf, lps, xw=xc[1],
                                         want_ev=True, bc=bc1)
            if not skip_ffn1:
                plan_ffn(p, b, sb, a_bf, dram, first_call=True, a_ready_dve=a_ready,
                         a_ev=a_ev1)
            with ExitStack() as _ps:
                lps = [_ps.enter_context(nc.psum_tensor("lnq%d" % i, [128, 512], F32)) for i in range(8)]
                a2_ready, a_ev2 = plan_ln(p, b, a_bf, lps, want_ev=True, bc=bc1,
                                          order="th")
            a_st_a = plan_store_a(p, a_bf, dram["ag1_in_a"], 0, "sync", a_ev2[(DC - 1, 0)])
            ag1a = plan_collective(p, "AllGather", [dram["ag1_in_a"][:, :]],
                                   [dram["ag1_out_a"][:, :]], "cc1a",
                                   wait_dma=("a_st0", a_st_a))
            a_st_b = plan_store_a(p, a_bf, dram["ag1_in_b"], 1, "gpsimd", a2_ready)
            dram["_a_st_a"] = a_st_a
            dram["_a_st_b"] = a_st_b
            if stop_after == "ffn1":
                finish(p, b, out_fm)
                return nc
        ag1_done = (ag1a,
                    plan_collective(p, "AllGather", [dram["ag1_in_b"][:, :]],
                                    [dram["ag1_out_b"][:, :]], "cc1b",
                                    wait_dma=("a_st1", a_st_b)))

        with ExitStack() as f2:
            at = {
                "wq": f2.enter_context(nc.sbuf_tensor("wq", [128, DC * 384], BF16)),
                "chunk": f2.enter_context(nc.sbuf_tensor("chunk", [128, 2 * 8192], BF16)),
                "q_fm": f2.enter_context(nc.sbuf_tensor("q_fm", [128, S], BF16)),
                "k_fm": f2.enter_context(nc.sbuf_tensor("k_fm", [128, S], BF16)),
                "v_tm": f2.enter_context(nc.sbuf_tensor("v_tm", [128, S], BF16)),
                "v_st": f2.enter_context(nc.sbuf_tensor("v_st", [128, 1024], BF16)),
                "exp": f2.enter_context(nc.sbuf_tensor("exp_sb", [128, 12 * 512], BF16)),

                "masks": f2.enter_context(nc.sbuf_tensor("masks", [128, 4 * 512 + 256], BF16)),
                "identf": f2.enter_context(nc.sbuf_tensor("identf", [128, 128], F32)),
                "gbpad": f2.enter_context(nc.sbuf_tensor("gbpad", [128, 128], F32)),
                "gb": f2.enter_context(nc.sbuf_tensor("gb", [128, 4], F32)),
                "lamc": f2.enter_context(nc.sbuf_tensor("lamc", [128, 1], F32)),
            }
            at["ident_bf"] = at["masks"][:, 4 * 512:4 * 512 + 128]
            at["tri_bf"] = at["masks"][:, 4 * 512 + 128:4 * 512 + 256]
            at["head_bf"] = at["v_st"]
            at["o"] = at["chunk"][:, 0:2048].bitcast(F32)
            at["on"] = at["chunk"][:, 2048:4096].bitcast(F32)
            at["accbf"] = at["chunk"][:, 4096:5120]
            at["acc"] = at["chunk"][:, 5120:7168].bitcast(F32)
            for hl in range(2):
                if hl == 0:
                    with nc.psum_tensor("qs_prep", [128, 512], F32) as ps_sp:
                        at["ps_sp"] = ps_sp
                        c_msk = plan_attn_prep(p, b, at, dram)
                        p.wait("vector", "prep2", c_msk)
                with ExitStack() as _ps:
                    psq = {"T": [[_ps.enter_context(nc.psum_tensor("q%d%d_%d" % (m, pi, hl), [128, 512], F32))
                                  for pi in range(2)] for m in range(3)],
                           "ps_t": [_ps.enter_context(nc.psum_tensor("qt%d_%d" % (pi, hl), [128, 1024], BF16))
                                    for pi in range(2)]}
                    p.wait("tensor", "dve", p.val("dve"))
                    p.wait("tensor", "act", p.val("act"))
                    p.wait("tensor", "mA", p.val("mA"))
                    qkv_done = plan_qkv_head(p, b, at, hl, dram, psq, ag1_done)
                if stop_after == "qkv":
                    for i, t in enumerate(("q_fm", "k_fm", "v_tm")):
                        p.wait("sync", "dve", p.val("dve"))
                        p.wait("sync", "act", p.val("act"))
                        p.dma("sync", dbg[i * 128:(i + 1) * 128, :], at[t][:, :], "s_out")
                    finish(p, b, out_fm)
                    return nc
                with ExitStack() as _ps:
                    sA = [_ps.enter_context(nc.psum_tensor("aA%d_%d" % (i, hl), [128, 512], F32)) for i in range(2)]
                    sB = [_ps.enter_context(nc.psum_tensor("aB%d_%d" % (i, hl), [128, 512], F32)) for i in range(2)]
                    aC = [_ps.enter_context(nc.psum_tensor("aC%d_%d" % (i, hl), [128, 512], F32)) for i in range(2)]
                    dD = [_ps.enter_context(nc.psum_tensor("aD%d_%d" % (i, hl), [128, 512], F32)) for i in range(2)]
                    plan_attn_head(p, b, at, hl, dram, (sA, sB, aC, dD), qkv_done, dbg=dbg)
            p.wait("gpsimd", "h_st0", p.val("h_st0"))
            p.wait("gpsimd", "h_st1", p.val("h_st1"))
            plan_collective(p, "AllToAll", [dram["a2a_in"][:, :]],
                            [dram["a2a_out"][:, :]], "cc2", wait_dma=None)
        plan_wo(p, b, dram, None)
        if stop_after == "wo":
            finish(p, b, out_fm)
            return nc

        with ExitStack() as f3:
            a_bf = f3.enter_context(nc.sbuf_tensor("a3_bf", [128, DC * TPC], BF16))
            sb = {
                "wgu": f3.enter_context(nc.sbuf_tensor("wgu2", [128, 4 * 2 * D], BF16)),
                "silu": f3.enter_context(nc.sbuf_tensor("silu2", [128, 2 * TPC], BF16)),
                "gated": f3.enter_context(nc.sbuf_tensor("gated2", [128, 2 * TPC], BF16)),
                "prep": None,
                "wd_in": f3.enter_context(nc.sbuf_tensor("wd_in2", [128, 2 * 8 * 512], BF16)),
                "g_in": f3.enter_context(nc.sbuf_tensor("g_in2", [128, 2 * 4 * 1024], BF16)),
            }
            p.wait("scalar", "dve", p.val("dve"))
            p.wait("tensor", "dve", p.val("dve"))
            with ExitStack() as _ps:
                lps = [_ps.enter_context(nc.psum_tensor("lnr%d" % i, [128, 512], F32)) for i in range(8)]
                bc3 = sb["g_in"][:, 0:4096].bitcast(F32)
                a3_ready, a_ev3 = plan_ln(p, b, a_bf, lps, want_ev=True, bc=bc3)
            fin = plan_ffn(p, b, sb, a_bf, dram, first_call=False, a_ready_dve=a3_ready,
                           out_stream=out_fm, a_ev=a_ev3)

        finish(p, b, out_fm, streamed=True)
    return nc


def finish(p, b, out_fm, streamed=False):
    p.wait("sync", "dve", p.val("dve"))
    p.wait("sync", "act", p.val("act"))
    if not streamed:
        src = b.x[:, :].rearrange("p (dc t) -> p dc t", t=TPC)
        dst = out_fm[:, :].rearrange("(dc p) t -> p dc t", p=128)
        p.dma("sync", dst, src, "s_out")
    p.wait("sync", "s_out", p.val("s_out"))
    p.wait("sync", "s_out_s", p.val("s_out_s"))
    p.check_deadlock()
    p.emit()


def make_host_inputs(inp):
    x = np.asarray(inp["L0_attn_x"])[0]
    W_qkv = np.asarray(inp["L0_attn_W_qkv"])
    W_o = np.asarray(inp["L0_attn_W_o"])
    wg = np.asarray(inp["L0_ffn_w_gate"])
    wu = np.asarray(inp["L0_ffn_w_up"])
    wd = np.asarray(inp["L0_ffn_w_down"])
    lq1 = np.asarray(inp["L0_attn_lambda_q1"])
    lk1 = np.asarray(inp["L0_attn_lambda_k1"])
    lq2 = np.asarray(inp["L0_attn_lambda_q2"])
    lk2 = np.asarray(inp["L0_attn_lambda_k2"])
    gamma = np.asarray(inp["L0_attn_gn_gamma"])
    beta = np.asarray(inp["L0_attn_gn_beta"])

    def tile_w(w):
        ho = w.shape[0] // 128
        t = w.reshape(ho, 128, w.shape[1] // 128, 128).transpose(0, 3, 2, 1)
        return np.ascontiguousarray(t.reshape(ho * 128, w.shape[1]))
    wgT = tile_w(wg)
    wuT = tile_w(wu)
    wdT = np.ascontiguousarray(wd.T)
    woT = tile_w(W_o)
    lam_row = np.concatenate([lq1, lk1, lq2, lk2]).astype(np.float32).reshape(1, 512)
    masks = np.zeros((128, 4 * 512 + 256), dtype=ml_dtypes.bfloat16)
    r = np.arange(128)[:, None]
    c = np.arange(512)[None, :]
    for m in range(4):
        masks[:, m * 512:(m + 1) * 512] = np.where((128 * m + r) > c, -400.0, 0.0).astype(ml_dtypes.bfloat16)
    masks[:, 4 * 512:4 * 512 + 128] = np.eye(128, dtype=ml_dtypes.bfloat16)
    masks[:, 4 * 512 + 128:4 * 512 + 256] = (r <= np.arange(128)[None, :]).astype(ml_dtypes.bfloat16)
    ident_f32 = np.eye(128, dtype=np.float32)

    qkv3 = W_qkv.reshape(3, 16, 128, D)
    in_maps = []
    for core in range(NCORES):
        xs = np.ascontiguousarray(x[core * TPC:(core + 1) * TPC, :].T)
        blk = np.empty((2 * 128, DC * 384), dtype=np.float32)
        for hl in range(2):
            h = 2 * core + hl
            for mat in range(3):
                t = qkv3[mat, h].reshape(128, DC, 128).transpose(2, 1, 0)
                blk[hl * 128:(hl + 1) * 128, :].reshape(128, DC, 3, 128)[:, :, mat, :] = t
        gn4 = np.stack([gamma[2 * core], beta[2 * core],
                        gamma[2 * core + 1], beta[2 * core + 1]]).astype(np.float32)
        in_maps.append({
            "x_fm": xs, "wgT": wgT, "wuT": wuT, "wdT": wdT,
            "wqkvT": np.ascontiguousarray(blk), "woT": woT,
            "lam_row": lam_row, "gn4": gn4,
            "consts_bf": masks, "ident_f32": ident_f32,
        })
    return in_maps


def kernel(**inputs):
    nc = build_full(os.environ.get("MK_STOP", "full"))
    in_maps = make_host_inputs(inputs)
    res = run_bass_kernel_spmd(nc, in_maps, core_ids=list(range(NCORES)),
                               trace=os.environ.get("MK_TRACE", "0") == "1")
    out = np.empty((1, S, D), dtype=np.float32)
    for core in range(NCORES):
        out[0, core * TPC:(core + 1) * TPC, :] = res.results[core]["out_fm"].T
    return out


def np_ln(x):
    x = x.astype(np.float64)
    mu = x.mean(-1, keepdims=True)
    v = x.var(-1, keepdims=True)
    return (x - mu) / np.sqrt(v + LN_EPS)


def np_ffn_res(x, wg, wu, wd):
    xn = np_ln(x)
    g = xn @ wg.T.astype(np.float64)
    u = xn @ wu.T.astype(np.float64)
    gated = (g / (1.0 + np.exp(-g))) * u
    return x + 0.5 * (gated @ wd.T.astype(np.float64))



if __name__ == "__main__":
    pass
